# version 05
# baseline (speedup 1.0000x reference)
import numpy as np
from contextlib import ExitStack

import concourse.bass as bass
import concourse.mybir as mybir
from concourse.bass_utils import run_bass_kernel_spmd

F32 = mybir.dt.float32
BF16 = mybir.dt.bfloat16
AF = mybir.ActivationFunctionType
ALU = mybir.AluOpType
AX = mybir.AxisListType

ENGS = ("sync", "scalar", "vector", "tensor", "gpsimd")

D = 2048
HD = 128
NCH = D // 128
NHL = 4
ISH = 5632
NICS = ISH // 128
IMOE = 1408
NICM = IMOE // 128
NEXP = 8
NICS_C = 44
NICM_C = 11
JBS_C = 256
NJBS_C = 2048 // JBS_C
JBM_C = 512
NJBM_C = 2048 // JBM_C
SCALE = float(np.float32(1.0) / np.float32(np.sqrt(np.float32(128.0))))
NEGBIG = -30000.0
EPS = 1e-5

SEMS = ["vv", "ss", "pp", "cc", "sc0",
        "sx", "gw", "swk0", "swk1", "swq0", "swq1", "skh", "sqh",
        "spw0", "spw1", "srs", "sxo",
        "sx1", "sr10", "sr11", "sx20", "sx21", "sr2", "sout",
        "gwg0", "gwg1", "gwu0", "gwu1", "gwd0", "gwd1"]


def _r3(src2d, p=128):
    return src2d.rearrange("(c p) n -> p c n", p=p)


def _s3(sb, c):
    return sb.ap().rearrange("p (c n) -> p c n", c=c)


class K:

    def __init__(self, nc):
        self.nc = nc
        self.cnt = {s: 0 for s in SEMS}
        self.lastw = {e: {} for e in ENGS}
        self.sems = {}
        self.ops = {e: [] for e in ENGS}
        self.pending_barrier = None
        self.barrier_done = set()

    def _op(self, eng, fn, waits, inc, k):
        wl = []
        if self.pending_barrier is not None and eng not in self.barrier_done:
            waits = self.pending_barrier + list(waits)
            self.barrier_done.add(eng)
        lw = self.lastw[eng]
        for s, c in waits:
            if c > 0 and lw.get(s, 0) < c:
                wl.append((s, c))
                lw[s] = c
        self.ops[eng].append((fn, wl, inc, k))
        if inc is None:
            return None
        self.cnt[inc] += k
        return self.cnt[inc]

    def v(self, fn, waits=()):
        return self._op("vector", fn, list(waits) + [("vv", self.cnt["vv"])], "vv", 1)

    def s(self, fn, waits=()):
        return self._op("scalar", fn, list(waits) + [("ss", self.cnt["ss"])], "ss", 1)

    def t(self, fn, waits=(), mark=False):
        return self._op("tensor", fn, waits, "pp" if mark else None, 1)

    def dma(self, eng, sem, fn, waits=()):
        return self._op(eng, fn, waits, sem, 16)

    def coll(self, fn, waits=()):
        return self._op("gpsimd", fn, waits, "cc", 1)

    def snap(self, extra=()):
        out = [(s, self.cnt[s]) for s in ("vv", "ss", "pp")]
        for s in extra:
            out.append((s, self.cnt[s]))
        return out

    def flush(self, barrier=True):
        nc = self.nc
        ops = self.ops
        sems = self.sems
        with nc.Block() as blk:
            for e in ENGS:
                lst = ops[e]
                if not lst:
                    continue

                def runner(eng, lst=lst):
                    for fn, wl, inc, k in lst:
                        for s, c in wl:
                            eng.wait_ge(sems[s], c)
                        inst = fn(eng)
                        if inc is not None:
                            inst.then_inc(sems[inc], k)

                getattr(blk, e)(runner)
        self.ops = {e: [] for e in ENGS}
        if barrier:
            self.pending_barrier = [(s, self.cnt[s]) for s in SEMS if self.cnt[s] > 0]
            self.barrier_done = set()


def emit_ln(kb, xt, tmp, out_ap, x_waits, d_inv):
    m_sum = kb.v(lambda e, a=tmp["s1"], b=xt: e.reduce_sum(a[:, :], b, axis=AX.X),
                 waits=x_waits)
    m_sq = kb.s(lambda e, o=tmp["sq"], i=xt, a=tmp["s2"]:
                e.activation(o[:, :], i, AF.Square, accum_out=a[:, :]),
                waits=x_waits)
    kb.v(lambda e, o=tmp["mean"], i=tmp["s1"]: e.tensor_scalar_mul(o[:, :], i[:, :], d_inv))
    kb.v(lambda e, o=tmp["m2"], i=tmp["mean"]: e.tensor_mul(o[:, :], i[:, :], i[:, :]))
    kb.v(lambda e, o=tmp["var"], i=tmp["s2"], j=tmp["m2"]:
         e.scalar_tensor_tensor(o[:, :], i[:, :], d_inv, j[:, :], ALU.mult, ALU.subtract),
         waits=[("ss", m_sq)])
    m_var = kb.cnt["vv"]
    m_sqrt = kb.s(lambda e, o=tmp["rstd0"], i=tmp["var"], b=kb.eps:
                  e.activation(o[:, :], i[:, :], AF.Sqrt, bias=b[:, 0:1]),
                  waits=[("vv", m_var)])
    kb.v(lambda e, o=tmp["rstd"], i=tmp["rstd0"]: e.reciprocal(o[:, :], i[:, :]),
         waits=[("ss", m_sqrt)])
    m_norm = kb.v(lambda e, o=out_ap, i=xt, m=tmp["mean"], r=tmp["rstd"]:
                  e.tensor_scalar(o, i, m[:, 0:1], r[:, 0:1], ALU.subtract, ALU.mult))
    return m_norm


def build_nc(T=8192, detect_race=True):
    NSUP = T // 512
    NQB = T // 512
    NKCH = T // 128
    n_own = T // 4
    NTS = max(1, n_own // 512)
    HF = n_own // 2
    SUPF = min(512, HF)
    NTQ = HF // SUPF
    NTCH = HF // 128
    JBS = JBS_C
    NJBS = NJBS_C
    JBM = JBM_C
    NJBM = NJBM_C
    d_inv = 1.0 / D

    nc = bass.Bass("TRN2", target_bir_lowering=False, debug=False,
                   detect_race_conditions=detect_race)

    x_in = nc.dram_tensor("x", [T, D], F32, kind="ExternalInput").ap()
    xo_in = nc.dram_tensor("x_own", [n_own, D], F32, kind="ExternalInput").ap()
    wq_in = nc.dram_tensor("wq", [128, NCH * 512], F32, kind="ExternalInput").ap()
    wk_in = nc.dram_tensor("wk", [128, NCH * 512], F32, kind="ExternalInput").ap()
    wv_in = nc.dram_tensor("wv", [128, NCH * 512], F32, kind="ExternalInput").ap()
    wos_in = nc.dram_tensor("wos", [128, NHL * D], F32, kind="ExternalInput").ap()
    wrsg_in = nc.dram_tensor("wrsg", [128, NCH * 9], F32, kind="ExternalInput").ap()
    wsgu_in = nc.dram_tensor("wsgu", [2 * NICS * 128, NCH * 128], F32, kind="ExternalInput").ap()
    wsd_in = nc.dram_tensor("wsd", [NJBS_C * 128, NICS * JBS_C], F32, kind="ExternalInput").ap()
    weg_in = nc.dram_tensor("weg", [NEXP * 2 * NICM * 128, NCH * 128], F32, kind="ExternalInput").ap()
    wed_in = nc.dram_tensor("wed", [NEXP * NJBM_C * 128, NICM * JBM_C], F32, kind="ExternalInput").ap()
    ident_in = nc.dram_tensor("ident", [128, 128], F32, kind="ExternalInput").ap()
    onesm_in = nc.dram_tensor("ones_mat", [128, 128], F32, kind="ExternalInput").ap()
    strip_in = nc.dram_tensor("strip", [128, 896], F32, kind="ExternalInput").ap()
    out_t = nc.dram_tensor("out", [n_own, D], F32, kind="ExternalOutput").ap()

    k_scr = nc.dram_tensor("k_scr", [512, T], BF16).ap()
    q_scr = nc.dram_tensor("q_scr", [512, T], BF16).ap()
    part_scr = nc.dram_tensor("part_scr", [4 * T, 512], BF16).ap()
    rs_scr = nc.dram_tensor("rs_scr", [4 * n_own, 512], BF16).ap()
    x1_scr = nc.dram_tensor("x1_scr", [n_own, D], F32).ap()
    x2_scr = nc.dram_tensor("x2_scr", [n_own, D], F32).ap()

    kb = K(nc)

    with ExitStack() as outer:
        E = outer.enter_context
        for s in SEMS:
            kb.sems[s] = E(nc.semaphore(s))

        ident = E(nc.sbuf_tensor("ident_bf", [128, 128], BF16))
        onesm = E(nc.sbuf_tensor("onesm_bf", [128, 128], BF16))
        strip = E(nc.sbuf_tensor("strip_f", [128, 896], F32))
        wrsg_f = E(nc.sbuf_tensor("wrsg_f", [128, NCH * 9], F32))
        wrhi = E(nc.sbuf_tensor("wrhi", [128, NCH * 9], BF16))
        wrhi32 = E(nc.sbuf_tensor("wrhi32", [128, NCH * 9], F32))
        wrlo = E(nc.sbuf_tensor("wrlo", [128, NCH * 9], BF16))
        w_all = E(nc.sbuf_tensor("w_all", [128, (n_own // 128) * 8], F32))
        eps_sb = E(nc.sbuf_tensor("eps_sb", [128, 1], F32))
        kb.eps = eps_sb
        sg_all = E(nc.sbuf_tensor("sg_all", [128, n_own // 128], F32))

        kb.dma("gpsimd", "gw", lambda e: e.dma_start(out=ident[:, :], in_=ident_in))
        kb.dma("gpsimd", "gw", lambda e: e.dma_start(out=onesm[:, :], in_=onesm_in))
        kb.dma("sync", "sc0", lambda e: e.dma_start(out=strip[:, :], in_=strip_in))
        c_wrsg = kb.dma("sync", "sc0", lambda e: e.dma_start(
            out=wrsg_f[:, :], in_=wrsg_in))
        kb.v(lambda e, o=eps_sb: e.memset(o[:, :], EPS))
        kb.v(lambda e: e.tensor_copy(wrhi[:, :], wrsg_f[:, :]), waits=[("sc0", c_wrsg)])
        kb.v(lambda e: e.tensor_copy(wrhi32[:, :], wrhi[:, :]))
        kb.v(lambda e: e.tensor_sub(wrlo[:, :], wrsg_f[:, :], wrhi32[:, :]))

        v_stack = ExitStack()
        v_sb = v_stack.enter_context(nc.sbuf_tensor("v_sb", [128, NHL * T], BF16))
        z_stack = ExitStack()

        with ExitStack() as sA:
            EA = sA.enter_context
            wqs = EA(nc.sbuf_tensor("wqs", [128, NCH * 512], BF16))
            wks = EA(nc.sbuf_tensor("wks", [128, NCH * 512], BF16))
            wvs = EA(nc.sbuf_tensor("wvs", [128, NCH * 512], BF16))
            x_sb = EA(nc.sbuf_tensor("x_sb", [128, 4 * D], F32))
            xln = EA(nc.sbuf_tensor("xln", [128, 4 * D], BF16))
            xfm = EA(nc.sbuf_tensor("xfm", [128, NCH * 512], BF16))
            sq_sb = EA(nc.sbuf_tensor("sq_sb", [128, D], F32))
            kst = [EA(nc.sbuf_tensor(f"kst{i}", [128, 512], BF16)) for i in range(2)]
            qst = [EA(nc.sbuf_tensor(f"qst{i}", [128, 512], BF16)) for i in range(2)]
            tmp = {n: EA(nc.sbuf_tensor(f"ln_{n}", [128, 1], F32))
                   for n in ("s1", "s2", "mean", "m2", "var", "rstd0", "rstd")}
            tmp["sq"] = sq_sb
            ps_t = [EA(nc.psum_tensor(f"ps_t{i}", [128, 128], BF16)) for i in range(2)]
            ps_kq = [EA(nc.psum_tensor(f"ps_kq{i}", [128, 512], F32)) for i in range(2)]
            ps_v = [EA(nc.psum_tensor(f"ps_v{i}", [128, 512], F32)) for i in range(2)]

            c_wq = kb.dma("gpsimd", "gw", lambda e: e.dma_start(
                out=wqs[:, :], in_=wq_in))
            c_wk = kb.dma("gpsimd", "gw", lambda e: e.dma_start(
                out=wks[:, :], in_=wk_in))
            c_wv = kb.dma("gpsimd", "gw", lambda e: e.dma_start(
                out=wvs[:, :], in_=wv_in))
            c_gw_all = kb.cnt["gw"]

            x_free = [(0, 0)]
            tposev_mark = [0, 0]
            kq_war = [0, 0]
            v_war = [0, 0]
            kst_war = [0, 0]
            qst_war = [0, 0]
            tr_cnt = 0

            for st_i in range(NSUP):
                t0 = st_i * 512
                xa = x_in[t0:t0 + 512, :].rearrange("(a p) d -> p a d", p=128)
                vvf, ssf = x_free[-1]
                c_x = kb.dma("sync", "sx",
                             lambda e, xa=xa, o=_s3(x_sb, 4): e.dma_start(out=o, in_=xa),
                             waits=[("vv", vvf), ("ss", ssf)])
                xw = [("sx", c_x)]
                norm_marks = []
                for a in range(4):
                    xt = x_sb[:, a * D:(a + 1) * D]
                    m_n = emit_ln(kb, xt, tmp, xln[:, a * D:(a + 1) * D], xw, d_inv)
                    norm_marks.append(m_n)
                x_free.append((kb.cnt["vv"], kb.cnt["ss"]))
                for a in range(4):
                    for c in range(NCH):
                        par = tr_cnt % 2
                        tr_cnt += 1
                        m_t = kb.t(lambda e, o=ps_t[par], i=xln[:, a * D + c * 128: a * D + (c + 1) * 128],
                                   idn=ident: e.transpose(o[:, :], i, idn[:, :]),
                                   waits=[("vv", norm_marks[a]), ("ss", tposev_mark[par]),
                                          ("gw", c_gw_all)],
                                   mark=True)
                        kb.s(lambda e, o=xfm[:, c * 512 + a * 128: c * 512 + (a + 1) * 128],
                             i=ps_t[par]: e.copy(o, i[:, :]),
                             waits=[("pp", m_t)])
                        tposev_mark[par] = kb.cnt["ss"]
                m_evac_all = kb.cnt["ss"]

                for w_sb, stg, scr, sems2, war in (
                        (wks, kst, k_scr, ("swk0", "swk1"), kst_war),
                        (wqs, qst, q_scr, ("swq0", "swq1"), qst_war)):
                    for j in range(4):
                        par = j % 2
                        for c in range(NCH):
                            kb.t(lambda e, o=ps_kq[par],
                                 l=w_sb[:, c * 512 + j * 128: c * 512 + (j + 1) * 128],
                                 r=xfm[:, c * 512:(c + 1) * 512],
                                 st=(c == 0), sp=(c == NCH - 1):
                                 e.matmul(o[:, :], l, r, start=st, stop=sp),
                                 waits=[("ss", m_evac_all), ("gw", c_gw_all),
                                        ("vv", kq_war[par])],
                                 mark=(c == NCH - 1))
                        m_mm = kb.cnt["pp"]
                        kb.v(lambda e, o=stg[par], i=ps_kq[par]: e.tensor_copy(o[:, :], i[:, :]),
                             waits=[("pp", m_mm), (sems2[par], war[par])])
                        m_ev = kb.cnt["vv"]
                        kq_war[par] = m_ev
                        war[par] = kb.dma("sync", sems2[par],
                                          lambda e, o=scr[j * 128:(j + 1) * 128, t0:t0 + 512],
                                          i=stg[par]: e.dma_start(out=o, in_=i[:, :]),
                                          waits=[("vv", m_ev)])
                for a in range(4):
                    par = a % 2
                    for c in range(NCH):
                        kb.t(lambda e, o=ps_v[par],
                             l=xfm[:, c * 512 + a * 128: c * 512 + (a + 1) * 128],
                             r=wvs[:, c * 512:(c + 1) * 512],
                             st=(c == 0), sp=(c == NCH - 1):
                             e.matmul(o[:, :], l, r, start=st, stop=sp),
                             waits=[("ss", m_evac_all), ("gw", c_gw_all), ("vv", v_war[par])],
                             mark=(c == NCH - 1))
                    m_mm = kb.cnt["pp"]
                    kc = t0 // 128 + a
                    for h in range(NHL):
                        kb.v(lambda e, o=v_sb[:, h * T + kc * 128:h * T + (kc + 1) * 128],
                             i=ps_v[par][:, h * 128:(h + 1) * 128]: e.tensor_copy(o, i),
                             waits=[("pp", m_mm)])
                    v_war[par] = kb.cnt["vv"]
            kb.flush()

        EZ = z_stack.enter_context
        z_sb = EZ(nc.sbuf_tensor("z_sb", [128, NHL * T], BF16, side="right"))
        wos_sb = EZ(nc.sbuf_tensor("wos_sb", [128, NHL * D], BF16, side="right"))

        with ExitStack() as sB:
            EB = sB.enter_context
            NR = 4
            kh = EB(nc.sbuf_tensor("kh", [128, T], BF16))
            qh = EB(nc.sbuf_tensor("qh", [128, T], BF16))
            p_ring = [EB(nc.sbuf_tensor(f"p_ring{i}", [128, 1024], BF16)) for i in range(NR)]
            rbc_sb = [EB(nc.sbuf_tensor(f"rbc_sb{i}", [128, 512], F32)) for i in range(2)]
            ps_s = [EB(nc.psum_tensor(f"ps_s{i}", [128, 1024], F32)) for i in range(2)]
            ps_o = [EB(nc.psum_tensor(f"ps_o{i}", [128, 512], F32)) for i in range(2)]
            ps_lb = [EB(nc.psum_tensor(f"ps_lb{i}", [128, 512], F32)) for i in range(2)]

            c_wos = kb.dma("gpsimd", "gw", lambda e: e.dma_start(
                out=wos_sb[:, :], in_=wos_in))

            exp_mark = {0: 0, 1: 0}
            ring_free = {}
            o_war = [0, 0]
            lb_war = [0, 0]
            pg = 0
            c_kh = c_qh = 0

            def emit_avl(pend, opar, nk, h):
                slot, kc0, m_exp = pend
                for half in (0, 1):
                    kc = kc0 + half
                    kb.t(lambda e, o=ps_o[opar],
                         l=v_sb[:, h * T + kc * 128:h * T + (kc + 1) * 128],
                         r=p_ring[slot][:, half * 512:(half + 1) * 512],
                         st=(kc == 0), sp=(kc == nk - 1):
                         e.matmul(o[:, :], l, r, start=st, stop=sp),
                         waits=[("ss", m_exp)] + ([("vv", o_war[opar])] if kc == 0 else []),
                         mark=(kc == nk - 1))
                    kb.t(lambda e, o=ps_lb[opar], l=onesm,
                         r=p_ring[slot][:, half * 512:(half + 1) * 512],
                         st=(kc == 0), sp=(kc == nk - 1):
                         e.matmul(o[:, :], l[:, :], r, start=st, stop=sp),
                         waits=[("ss", m_exp)] + ([("vv", lb_war[opar])] if kc == 0 else []),
                         mark=(half == 1))
                ring_free[slot] = kb.cnt["pp"]

            for h in range(NHL):
                c_kh = kb.dma("sync", "skh",
                              lambda e, o=kh, i=k_scr[h * 128:(h + 1) * 128, :]:
                              e.dma_start(out=o[:, :], in_=i),
                              waits=kb.snap(("swk0", "swk1")))
                c_qh = kb.dma("scalar", "sqh",
                              lambda e, o=qh, i=q_scr[h * 128:(h + 1) * 128, :]:
                              e.dma_start(out=o[:, :], in_=i),
                              waits=kb.snap(("swq0", "swq1")))
                for qb in range(NQB):
                    nk = 4 * (qb + 1)
                    opar = qb % 2
                    pend = None
                    for j in range(nk // 2):
                        kc0 = 2 * j
                        spar = pg % 2
                        slot = pg % NR
                        pg += 1
                        m_s = 0
                        for half in (0, 1):
                            kc = kc0 + half
                            m_s = kb.t(lambda e, o=ps_s[spar][:, half * 512:(half + 1) * 512],
                                       l=kh[:, kc * 128:(kc + 1) * 128],
                                       r=qh[:, qb * 512:(qb + 1) * 512]:
                                       e.matmul(o, l, r, start=True, stop=True),
                                       waits=[("skh", c_kh), ("sqh", c_qh),
                                              ("ss", exp_mark[spar])],
                                       mark=True)
                        mask_w = []
                        for half in (0, 1):
                            kc = kc0 + half
                            if kc >= 4 * qb:
                                s_off = 384 - 128 * (kc - 4 * qb)
                                kb.v(lambda e, o=ps_s[spar][:, half * 512:(half + 1) * 512],
                                     m=strip[:, s_off:s_off + 512]:
                                     e.tensor_tensor(o, o, m, ALU.add),
                                     waits=[("pp", m_s - 1 + half)])
                                mask_w = [("vv", kb.cnt["vv"])]
                        if pend is not None:
                            emit_avl(pend, opar, nk, h)
                        m_exp = kb.s(lambda e, o=p_ring[slot], i=ps_s[spar]:
                                     e.activation(o[:, :], i[:, :], AF.Exp, scale=SCALE),
                                     waits=[("pp", m_s), ("pp", ring_free.get(slot, 0))] + mask_w)
                        exp_mark[spar] = m_exp
                        pend = (slot, kc0, m_exp)
                    emit_avl(pend, opar, nk, h)
                    m_lend = kb.cnt["pp"]
                    kb.v(lambda e, o=rbc_sb[opar], i=ps_lb[opar]:
                         e.reciprocal(o[:, :], i[:, :]),
                         waits=[("pp", m_lend)])
                    lb_war[opar] = kb.cnt["vv"]
                    kb.v(lambda e, o=z_sb[:, h * T + qb * 512:h * T + (qb + 1) * 512],
                         i=ps_o[opar], j=rbc_sb[opar]:
                         e.tensor_tensor(o, i[:, :], j[:, :], ALU.mult))
                    o_war[opar] = kb.cnt["vv"]
            kb.flush()
        v_stack.close()

        with ExitStack() as sB2:
            EB2 = sB2.enter_context
            GRP = 4
            pstg = [EB2(nc.sbuf_tensor(f"pstg{i}", [128, GRP * 512], BF16)) for i in range(2)]
            ps_w = [EB2(nc.psum_tensor(f"ps_w{i}", [128, 512], F32)) for i in range(2)]
            w_war = [0, 0]
            stg_war = [0, 0]
            for j in range(4):
                for tc in range(NKCH):
                    par = tc % 2
                    slot = (tc // GRP) % 2
                    sem = ("spw0", "spw1")[slot]
                    for hl in range(NHL):
                        kb.t(lambda e, o=ps_w[par],
                             l=z_sb[:, hl * T + tc * 128:hl * T + (tc + 1) * 128],
                             r=wos_sb[:, hl * D + j * 512:hl * D + (j + 1) * 512],
                             st=(hl == 0), sp=(hl == NHL - 1):
                             e.matmul(o[:, :], l, r, start=st, stop=sp),
                             waits=[("gw", c_wos), ("vv", w_war[par])],
                             mark=(hl == NHL - 1))
                    m_mm = kb.cnt["pp"]
                    g = tc % GRP
                    kb.v(lambda e, o=pstg[slot][:, g * 512:(g + 1) * 512], i=ps_w[par]:
                         e.tensor_copy(o, i[:, :]),
                         waits=[("pp", m_mm)] + ([(sem, stg_war[slot])] if g == 0 else []))
                    w_war[par] = kb.cnt["vv"]
                    if g == GRP - 1:
                        base = j * T + (tc - GRP + 1) * 128
                        stg_war[slot] = kb.dma(
                            "sync", sem,
                            lambda e,
                            o=part_scr[base:base + GRP * 128, :].rearrange(
                                "(g p) n -> p g n", p=128),
                            i=pstg[slot].ap().rearrange("p (g n) -> p g n", g=GRP):
                            e.dma_start(out=o, in_=i),
                            waits=[("vv", w_war[par])])
                kb.coll(lambda e, j=j: e.collective_compute(
                    "ReduceScatter", ALU.add,
                    replica_groups=[[0, 1, 2, 3], [4, 5, 6, 7]],
                    ins=[part_scr[j * T:(j + 1) * T, :]],
                    outs=[rs_scr[j * n_own:(j + 1) * n_own, :]]),
                    waits=[("spw0", kb.cnt["spw0"]), ("spw1", kb.cnt["spw1"])])
            kb.flush()
        z_stack.close()

        xfm2 = E(nc.sbuf_tensor("xfm2", [128, NCH * n_own], BF16))

        up_stacks = {0: ExitStack(), 1: ExitStack()}

        def alloc_up(hf):
            EU = up_stacks[hf].enter_context
            return dict(
                wg=[EU(nc.sbuf_tensor(f"wg{i}_{hf}", [128, NCH * 128], BF16, side="right"))
                    for i in range(2)],
                wu=[EU(nc.sbuf_tensor(f"wu{i}_{hf}", [128, NCH * 128], BF16, side="right"))
                    for i in range(2)],
                silu=[EU(nc.sbuf_tensor(f"silu{i}_{hf}", [128, SUPF], F32, side="right"))
                      for i in range(2)],
                ps_g=[EU(nc.psum_tensor(f"ps_g{i}_{hf}", [128, SUPF], F32)) for i in range(2)],
                ps_u=[EU(nc.psum_tensor(f"ps_u{i}_{hf}", [128, SUPF], F32)) for i in range(2)])

        def emit_wload(bufs, ic, waits):
            ring = ic % 2
            c_g = kb.dma("gpsimd", ("gwg0", "gwg1")[ring],
                         lambda e, o=bufs["wg"][ring],
                         i=wsgu_in[ic * 128:(ic + 1) * 128, :]:
                         e.dma_start(out=o[:, :], in_=i),
                         waits=waits)
            c_u = kb.dma("gpsimd", ("gwu0", "gwu1")[ring],
                         lambda e, o=bufs["wu"][ring],
                         i=wsgu_in[(NICS + ic) * 128:(NICS + ic + 1) * 128, :]:
                         e.dma_start(out=o[:, :], in_=i),
                         waits=waits)
            return (c_g, c_u)

        def init_waits():
            return [(s, kb.cnt[s]) for s in
                    ("vv", "ss", "pp", "srs", "sxo", "sx1",
                     "sr10", "sr11", "sx20", "sx21")]

        up0 = alloc_up(0)
        pre0 = {ic: emit_wload(up0, ic, kb.snap()) for ic in (0, 1)}
        st_ss = {}
        x1_cnt = {}

        with ExitStack() as sC:
            EC = sC.enter_context
            rs_sb = EC(nc.sbuf_tensor("rs_sb", [128, D], BF16))
            rs32 = EC(nc.sbuf_tensor("rs32", [128, D], F32))
            xo_sb = EC(nc.sbuf_tensor("xo_sb", [128, D], F32))
            x1_sb = EC(nc.sbuf_tensor("x1_sb", [128, D], F32))
            xln32 = EC(nc.sbuf_tensor("xln32", [128, D], F32))
            xhi = EC(nc.sbuf_tensor("xhi", [128, D], BF16))
            xhi32 = EC(nc.sbuf_tensor("xhi32", [128, D], F32))
            xlo = EC(nc.sbuf_tensor("xlo", [128, D], BF16))
            xlo_fm = EC(nc.sbuf_tensor("xlo_fm", [128, D], BF16))
            sq_c = EC(nc.sbuf_tensor("sq_c", [128, D], F32))
            pr8 = EC(nc.sbuf_tensor("pr8", [128, 8], F32))
            eq8 = EC(nc.sbuf_tensor("eq8", [128, 8], F32))
            msk8 = EC(nc.sbuf_tensor("msk8", [128, 8], F32))
            mx1 = EC(nc.sbuf_tensor("mx1", [128, 1], F32))
            nmax = EC(nc.sbuf_tensor("nmax", [128, 1], F32))
            se8 = EC(nc.sbuf_tensor("se8", [128, 1], F32))
            rse = EC(nc.sbuf_tensor("rse", [128, 1], F32))
            m2r = EC(nc.sbuf_tensor("m2r", [128, 1], F32))
            tmpc = {n: EC(nc.sbuf_tensor(f"lnc_{n}", [128, 1], F32))
                    for n in ("s1", "s2", "mean", "m2", "var", "rstd0", "rstd")}
            tmpc["sq"] = sq_c
            ps_t2 = [EC(nc.psum_tensor(f"ps_t2{i}", [128, 128], BF16)) for i in range(2)]
            ps_lg = EC(nc.psum_tensor("ps_lg", [128, 9], F32))

            tposev_mark2 = [0, 0]
            lg_w_vv, lg_w_ss = 0, 0
            tr_cnt2 = 0
            for tsn in range(NTS):
                for tci in range(4 if n_own >= 512 else n_own // 128):
                    trow = tsn * 512 + tci * 128
                    tcg = trow // 128
                    free = kb.snap()
                    c_rs = 0
                    for j in range(4):
                        c_rs = kb.dma("scalar", "srs",
                                      lambda e, o=rs_sb[:, j * 512:(j + 1) * 512],
                                      i=rs_scr[j * n_own + trow:j * n_own + trow + 128, :]:
                                      e.dma_start(out=o, in_=i),
                                      waits=[("cc", 4), ("vv", lg_w_vv)])
                    c_xo = kb.dma("sync", "sxo",
                                  lambda e, o=xo_sb, i=xo_in[trow:trow + 128, :]:
                                  e.dma_start(out=o[:, :], in_=i),
                                  waits=[("vv", lg_w_vv)])
                    kb.v(lambda e, o=rs32, i=rs_sb: e.tensor_copy(o[:, :], i[:, :]),
                         waits=[("srs", c_rs)])
                    kb.v(lambda e, o=x1_sb, i=rs32, j=xo_sb: e.tensor_add(o[:, :], i[:, :], j[:, :]),
                         waits=[("sxo", c_xo), ("sx1", kb.cnt["sx1"])])
                    m_x1 = kb.cnt["vv"]
                    c_x1 = kb.dma("sync", "sx1",
                                  lambda e, o=x1_scr[trow:trow + 128, :], i=x1_sb:
                                  e.dma_start(out=o, in_=i[:, :]),
                                  waits=[("vv", m_x1)])
                    emit_ln(kb, x1_sb[:, :], tmpc, xln32[:, :], [("vv", m_x1)], d_inv)
                    kb.v(lambda e, o=xhi, i=xln32: e.tensor_copy(o[:, :], i[:, :]))
                    kb.v(lambda e, o=xhi32, i=xhi: e.tensor_copy(o[:, :], i[:, :]))
                    kb.v(lambda e, o=xlo, i=xln32, j=xhi32: e.tensor_sub(o[:, :], i[:, :], j[:, :]))
                    m_hilo = kb.cnt["vv"]
                    for c in range(NCH):
                        for src, dst in ((xhi[:, c * 128:(c + 1) * 128],
                                          xfm2[:, c * n_own + trow:c * n_own + trow + 128]),
                                         (xlo[:, c * 128:(c + 1) * 128],
                                          xlo_fm[:, c * 128:(c + 1) * 128])):
                            par = tr_cnt2 % 2
                            tr_cnt2 += 1
                            m_t = kb.t(lambda e, o=ps_t2[par], i=src, idn=ident:
                                       e.transpose(o[:, :], i, idn[:, :]),
                                       waits=[("vv", m_hilo), ("ss", tposev_mark2[par])],
                                       mark=True)
                            kb.s(lambda e, o=dst, i=ps_t2[par]: e.copy(o, i[:, :]),
                                 waits=[("pp", m_t)])
                            tposev_mark2[par] = kb.cnt["ss"]
                    m_tev = kb.cnt["ss"]
                    n_mm = 0
                    for xsrc, wsrc in ((xfm2, wrhi), (xfm2, wrlo), (xlo_fm, wrhi)):
                        for c in range(NCH):
                            if xsrc is xfm2:
                                l_ap = xfm2[:, c * n_own + trow:c * n_own + trow + 128]
                            else:
                                l_ap = xlo_fm[:, c * 128:(c + 1) * 128]
                            st = (n_mm == 0)
                            sp = (n_mm == 3 * NCH - 1)
                            kb.t(lambda e, o=ps_lg, l=l_ap, r=wsrc[:, c * 9:(c + 1) * 9],
                                 st=st, sp=sp: e.matmul(o[:, :], l, r, start=st, stop=sp),
                                 waits=[("ss", m_tev), ("vv", lg_w_vv), ("ss", lg_w_ss)],
                                 mark=sp)
                            n_mm += 1
                    m_lg = kb.cnt["pp"]
                    kb.v(lambda e, o=mx1, i=ps_lg: e.reduce_max(o[:, :], i[:, 0:8], axis=AX.X),
                         waits=[("pp", m_lg)])
                    kb.v(lambda e, o=nmax, i=mx1: e.tensor_scalar_mul(o[:, :], i[:, :], -1.0))
                    m_nm = kb.cnt["vv"]
                    m_pe = kb.s(lambda e, o=pr8, i=ps_lg, b=nmax, a=se8:
                                e.activation(o[:, :], i[:, 0:8], AF.Exp,
                                             bias=b[:, 0:1], accum_out=a[:, :]),
                                waits=[("vv", m_nm), ("pp", m_lg)])
                    kb.v(lambda e, o=rse, i=se8: e.reciprocal(o[:, :], i[:, :]),
                         waits=[("ss", m_pe)])
                    kb.v(lambda e, o=pr8, i=pr8, r=rse:
                         e.tensor_scalar(o[:, :], i[:, :], r[:, 0:1], None, ALU.mult))
                    kb.v(lambda e, o=mx1, i=pr8: e.reduce_max(o[:, :], i[:, :], axis=AX.X))
                    kb.v(lambda e, o=eq8, i=pr8, m=mx1:
                         e.tensor_scalar(o[:, :], i[:, :], m[:, 0:1], None, ALU.is_equal))
                    kb.v(lambda e, o=eq8, i=pr8:
                         e.scalar_tensor_tensor(o[:, :], o[:, :], -1e30, i[:, :], ALU.mult, ALU.add))
                    kb.v(lambda e, o=m2r, i=eq8: e.reduce_max(o[:, :], i[:, :], axis=AX.X))
                    kb.v(lambda e, o=msk8, i=pr8, m=m2r:
                         e.tensor_scalar(o[:, :], i[:, :], m[:, 0:1], None, ALU.is_ge))
                    kb.v(lambda e, o=w_all[:, tcg * 8:(tcg + 1) * 8], i=pr8, m=msk8:
                         e.tensor_mul(o, i[:, :], m[:, :]))
                    lg_w_vv = kb.cnt["vv"]
                    lg_w_ss = kb.s(lambda e, o=sg_all[:, tcg:tcg + 1], i=ps_lg:
                                   e.activation(o, i[:, 8:9], AF.Sigmoid),
                                   waits=[("pp", m_lg)])
                st_ss[tsn] = kb.cnt["ss"]
                x1_cnt[tsn] = kb.cnt["sx1"]
            kb.flush(barrier=False)
        c_end = init_waits()

        with ExitStack() as sE:
            EE = sE.enter_context
            hsh = EE(nc.sbuf_tensor("hsh", [128, NICS * HF], BF16))
            for hf in range(2):
                hbase = hf * HF
                half_pp = kb.cnt["pp"]
                bufs = up0 if hf == 0 else alloc_up(1)
                wg_sb, wu_sb, silu_sb = bufs["wg"], bufs["wu"], bufs["silu"]
                ps_g, ps_u = bufs["ps_g"], bufs["ps_u"]
                up1_init = init_waits() if hf == 1 else None
                v0u = kb.cnt["vv"]
                p0u = kb.cnt["pp"]
                wld_war = [p0u, p0u]
                gu_war = [v0u, v0u]
                silu_war = [v0u, v0u]
                for ic in range(NICS):
                    ring = ic % 2
                    if hf == 0 and ic < 2:
                        c_g, c_u = pre0[ic]
                    else:
                        wload = up1_init if ic < 2 else [("pp", wld_war[ring])]
                        c_g, c_u = emit_wload(bufs, ic, wload)
                    for tq in range(NTQ):
                        tqb = hbase + tq * SUPF
                        par = tq % 2
                        for c in range(NCH):
                            kb.t(lambda e, o=ps_g[par],
                                 l=wg_sb[ring][:, c * 128:(c + 1) * 128],
                                 r=xfm2[:, c * n_own + tqb:c * n_own + tqb + SUPF],
                                 st=(c == 0), sp=(c == NCH - 1):
                                 e.matmul(o[:, :], l, r, start=st, stop=sp),
                                 waits=[(("gwg0", "gwg1")[ring], c_g), ("vv", gu_war[par]),
                                        ("ss", st_ss[hf * 2 + tq])],
                                 mark=(c == NCH - 1))
                        m_g = kb.cnt["pp"]
                        for c in range(NCH):
                            kb.t(lambda e, o=ps_u[par],
                                 l=wu_sb[ring][:, c * 128:(c + 1) * 128],
                                 r=xfm2[:, c * n_own + tqb:c * n_own + tqb + SUPF],
                                 st=(c == 0), sp=(c == NCH - 1):
                                 e.matmul(o[:, :], l, r, start=st, stop=sp),
                                 waits=[(("gwu0", "gwu1")[ring], c_u), ("vv", gu_war[par]),
                                        ("ss", st_ss[hf * 2 + tq])],
                                 mark=(c == NCH - 1))
                        m_u = kb.cnt["pp"]
                        if tq == NTQ - 1:
                            wld_war[ring] = m_u
                        m_si = kb.s(lambda e, o=silu_sb[par], i=ps_g[par]:
                                    e.activation(o[:, :], i[:, :], AF.Silu),
                                    waits=[("pp", m_g), ("vv", silu_war[par])])
                        kb.v(lambda e, o=hsh[:, ic * HF + tq * SUPF:ic * HF + (tq + 1) * SUPF],
                             i=ps_u[par], j=silu_sb[par]:
                             e.tensor_tensor(o, i[:, :], j[:, :], ALU.mult),
                             waits=[("pp", m_u), ("ss", m_si), ("pp", half_pp)])
                        gu_war[par] = kb.cnt["vv"]
                        silu_war[par] = gu_war[par]
                m_hsh = kb.cnt["vv"]
                up_stacks[hf].close()
                with ExitStack() as sDn:
                    ED = sDn.enter_context
                    ICQ = NICS // 4
                    wsdb = ED(nc.sbuf_tensor(f"wsdb_{hf}", [128, NICS * JBS], BF16))
                    x1s = [ED(nc.sbuf_tensor(f"x1s{i}_{hf}", [128, JBS], F32)) for i in range(2)]
                    x2s = [ED(nc.sbuf_tensor(f"x2s{i}_{hf}", [128, JBS], F32)) for i in range(2)]
                    ps_d = [ED(nc.psum_tensor(f"ps_d{i}_{hf}", [128, JBS], F32)) for i in range(2)]
                    dn_init = c_end if hf == 0 else up1_init
                    v0d = kb.cnt["vv"]
                    wd_war = 0
                    d_war = [v0d, v0d]
                    x1_war = [v0d, v0d]
                    x2_war = [kb.cnt["sx20"], kb.cnt["sx21"]]
                    for jb in range(NJBS):
                        wload = dn_init if jb == 0 else [("pp", wd_war)]
                        c_dq = []
                        for q in range(4):
                            c_dq.append(kb.dma(
                                "gpsimd", "gwd0",
                                lambda e, o=wsdb[:, q * ICQ * JBS:(q + 1) * ICQ * JBS],
                                i=wsd_in[jb * 128:(jb + 1) * 128,
                                         q * ICQ * JBS:(q + 1) * ICQ * JBS]:
                                e.dma_start(out=o, in_=i),
                                waits=wload if q == 0 else ()))
                        for tc in range(NTCH):
                            par = tc % 2
                            trow = hbase + tc * 128
                            tcg = trow // 128
                            c_x1 = kb.dma("scalar", ("sr10", "sr11")[par],
                                          lambda e, o=x1s[par],
                                          i=x1_scr[trow:trow + 128, jb * JBS:(jb + 1) * JBS]:
                                          e.dma_start(out=o[:, :], in_=i),
                                          waits=(dn_init if jb == 0 and tc < 2 else []) +
                                                [("vv", x1_war[par]),
                                                 ("sx1", x1_cnt[hf * 2 + 1])])
                            for ic in range(NICS):
                                kb.t(lambda e, o=ps_d[par],
                                     l=hsh[:, ic * HF + tc * 128:ic * HF + (tc + 1) * 128],
                                     r=wsdb[:, ic * JBS:(ic + 1) * JBS],
                                     st=(ic == 0), sp=(ic == NICS - 1):
                                     e.matmul(o[:, :], l, r, start=st, stop=sp),
                                     waits=[("gwd0", c_dq[min(ic // ICQ, 3)]),
                                            ("vv", m_hsh), ("vv", d_war[par])],
                                     mark=(ic == NICS - 1))
                            m_mm = kb.cnt["pp"]
                            if tc == NTCH - 1:
                                wd_war = m_mm
                            kb.v(lambda e, o=x2s[par], i=ps_d[par],
                                 s1=sg_all[:, tcg:tcg + 1], j=x1s[par]:
                                 e.scalar_tensor_tensor(o[:, :], i[:, :], s1, j[:, :],
                                                        ALU.mult, ALU.add),
                                 waits=[("pp", m_mm), (("sr10", "sr11")[par], c_x1),
                                        (("sx20", "sx21")[par], x2_war[par])])
                            d_war[par] = kb.cnt["vv"]
                            x1_war[par] = d_war[par]
                            x2_war[par] = kb.dma(
                                "sync", ("sx20", "sx21")[par],
                                lambda e, o=x2_scr[trow:trow + 128, jb * JBS:(jb + 1) * JBS],
                                i=x2s[par]: e.dma_start(out=o, in_=i[:, :]),
                                waits=[("vv", d_war[par])])
            kb.flush()

        with ExitStack() as sF:
            EF = sF.enter_context
            acc = EF(nc.sbuf_tensor("acc", [128, NTCH * D], F32))
            he = EF(nc.sbuf_tensor("he", [128, NICM * HF], BF16))
            wg_sb = [EF(nc.sbuf_tensor(f"mwg{i}", [128, NCH * 128], BF16)) for i in range(2)]
            wu_sb = [EF(nc.sbuf_tensor(f"mwu{i}", [128, NCH * 128], BF16)) for i in range(2)]
            silu_m = [EF(nc.sbuf_tensor(f"msilu{i}", [128, SUPF], F32)) for i in range(2)]
            wedb = [EF(nc.sbuf_tensor(f"wedb{i}", [128, NICM * JBM], BF16)) for i in range(2)]
            ps_g = [EF(nc.psum_tensor(f"mps_g{i}", [128, SUPF], F32)) for i in range(2)]
            ps_u = [EF(nc.psum_tensor(f"mps_u{i}", [128, SUPF], F32)) for i in range(2)]
            ps_m = [EF(nc.psum_tensor(f"mps_m{i}", [128, JBM], F32)) for i in range(2)]
            out_war_v = 0
            for hf in range(2):
                hbase = hf * HF
                accw = kb.snap(("sout",)) if hf else kb.snap()
                c_acc = 0
                for tc in range(NTCH):
                    c_acc = kb.dma("scalar", "sr2",
                                   lambda e, o=acc[:, tc * D:(tc + 1) * D],
                                   i=x2_scr[hbase + tc * 128:hbase + (tc + 1) * 128, :]:
                                   e.dma_start(out=o, in_=i),
                                   waits=accw)
                g0 = kb.cnt["pp"]
                v0 = kb.cnt["vv"]
                wld_war = [g0, g0]
                gu_war = [v0, v0]
                silu_war = [v0, v0]
                wd_war = [g0, g0]
                d_war = [v0, v0]
                he_war = g0
                for ex in range(NEXP):
                    for ic in range(NICM):
                        ring = ic % 2
                        first = (hf == 0 and ex == 0 and ic < 2)
                        wload = kb.snap() if first else [("pp", wld_war[ring])]
                        c_g = kb.dma("gpsimd", ("gwg0", "gwg1")[ring],
                                     lambda e, o=wg_sb[ring],
                                     i=weg_in[(ex * 2 * NICM + ic) * 128:(ex * 2 * NICM + ic + 1) * 128, :]:
                                     e.dma_start(out=o[:, :], in_=i),
                                     waits=wload)
                        c_u = kb.dma("gpsimd", ("gwu0", "gwu1")[ring],
                                     lambda e, o=wu_sb[ring],
                                     i=weg_in[(ex * 2 * NICM + NICM + ic) * 128:(ex * 2 * NICM + NICM + ic + 1) * 128, :]:
                                     e.dma_start(out=o[:, :], in_=i),
                                     waits=wload)
                        for tq in range(NTQ):
                            tqb = hbase + tq * SUPF
                            par = tq % 2
                            for c in range(NCH):
                                kb.t(lambda e, o=ps_g[par],
                                     l=wg_sb[ring][:, c * 128:(c + 1) * 128],
                                     r=xfm2[:, c * n_own + tqb:c * n_own + tqb + SUPF],
                                     st=(c == 0), sp=(c == NCH - 1):
                                     e.matmul(o[:, :], l, r, start=st, stop=sp),
                                     waits=[(("gwg0", "gwg1")[ring], c_g), ("vv", gu_war[par]), ("pp", he_war)],
                                     mark=(c == NCH - 1))
                            m_g = kb.cnt["pp"]
                            for c in range(NCH):
                                kb.t(lambda e, o=ps_u[par],
                                     l=wu_sb[ring][:, c * 128:(c + 1) * 128],
                                     r=xfm2[:, c * n_own + tqb:c * n_own + tqb + SUPF],
                                     st=(c == 0), sp=(c == NCH - 1):
                                     e.matmul(o[:, :], l, r, start=st, stop=sp),
                                     waits=[(("gwu0", "gwu1")[ring], c_u), ("vv", gu_war[par]), ("pp", he_war)],
                                     mark=(c == NCH - 1))
                            m_u = kb.cnt["pp"]
                            if tq == NTQ - 1:
                                wld_war[ring] = m_u
                            m_si = kb.s(lambda e, o=silu_m[par], i=ps_g[par]:
                                        e.activation(o[:, :], i[:, :], AF.Silu),
                                        waits=[("pp", m_g), ("vv", silu_war[par])])
                            kb.v(lambda e, o=he[:, ic * HF + tq * SUPF:ic * HF + (tq + 1) * SUPF],
                                 i=ps_u[par], j=silu_m[par]:
                                 e.tensor_tensor(o, i[:, :], j[:, :], ALU.mult),
                                 waits=[("pp", m_u), ("ss", m_si)])
                            gu_war[par] = kb.cnt["vv"]
                            silu_war[par] = gu_war[par]
                    m_he = kb.cnt["vv"]
                    for jb in range(NJBM):
                        ring = jb % 2
                        first = (hf == 0 and ex == 0 and jb < 2)
                        wload = kb.snap() if first else [("pp", wd_war[ring])]
                        c_d = kb.dma("gpsimd", ("gwd0", "gwd1")[ring],
                                     lambda e, o=wedb[ring],
                                     i=wed_in[(ex * NJBM + jb) * 128:(ex * NJBM + jb + 1) * 128, :]:
                                     e.dma_start(out=o[:, :], in_=i),
                                     waits=wload)
                        for tc in range(NTCH):
                            par = tc % 2
                            tcg = (hbase + tc * 128) // 128
                            for ic in range(NICM):
                                kb.t(lambda e, o=ps_m[par],
                                     l=he[:, ic * HF + tc * 128:ic * HF + (tc + 1) * 128],
                                     r=wedb[ring][:, ic * JBM:(ic + 1) * JBM],
                                     st=(ic == 0), sp=(ic == NICM - 1):
                                     e.matmul(o[:, :], l, r, start=st, stop=sp),
                                     waits=[(("gwd0", "gwd1")[ring], c_d), ("vv", m_he), ("vv", d_war[par])],
                                     mark=(ic == NICM - 1))
                            m_mm = kb.cnt["pp"]
                            if tc == NTCH - 1:
                                wd_war[ring] = m_mm
                            kb.v(lambda e, o=acc[:, tc * D + jb * JBM:tc * D + (jb + 1) * JBM],
                                 i=ps_m[par], s1=w_all[:, tcg * 8 + ex:tcg * 8 + ex + 1],
                                 a2=acc[:, tc * D + jb * JBM:tc * D + (jb + 1) * JBM]:
                                 e.scalar_tensor_tensor(o, i[:, :], s1, a2, ALU.mult, ALU.add),
                                 waits=[("pp", m_mm), ("sr2", c_acc)])
                            d_war[par] = kb.cnt["vv"]
                    he_war = max(wd_war)
                m_done = kb.cnt["vv"]
                for tc in range(NTCH):
                    trow = hbase + tc * 128
                    kb.dma("sync", "sout",
                           lambda e, o=out_t[trow:trow + 128, :], i=acc[:, tc * D:(tc + 1) * D]:
                           e.dma_start(out=o, in_=i),
                           waits=[("vv", m_done)])
                out_war_v = m_done
            kb._op("sync", lambda e: e.nop(), [("sout", kb.cnt["sout"])], None, 1)
            kb.flush(barrier=False)

    return nc


def _prep_core_inputs(core, x_np, Wq, Wk, Wv, Wo, Wr, Wsg, Wsgu, Wsd, Weg, Wed, T):
    b = core // 4
    hq = core % 4
    n_own = T // 4
    own0 = hq * n_own
    cs = hq * 512
    f32 = np.float32
    strip = np.zeros((128, 896), f32)
    for kk in range(128):
        strip[kk, :kk + 384] = NEGBIG
    def img(a, nch=16):
        n = a.shape[1]
        return np.ascontiguousarray(
            a.reshape(nch, 128, n).transpose(1, 0, 2).reshape(128, nch * n))

    wsgu_t = Wsgu.reshape(16, 128, 2 * NICS_C, 128).transpose(2, 1, 0, 3) \
        .reshape(2 * NICS_C * 128, 16 * 128)
    wsd_t = Wsd.reshape(NICS_C, 128, NJBS_C, JBS_C).transpose(2, 1, 0, 3) \
        .reshape(NJBS_C * 128, NICS_C * JBS_C)
    weg_t = Weg.reshape(NEXP, 16, 128, 2 * 11, 128).transpose(0, 3, 2, 1, 4) \
        .reshape(NEXP * 2 * 11 * 128, 16 * 128)
    wed_t = Wed.reshape(NEXP, 11, 128, NJBM_C, JBM_C).transpose(0, 3, 2, 1, 4) \
        .reshape(NEXP * NJBM_C * 128, 11 * JBM_C)
    d = {
        "x": np.ascontiguousarray(x_np[b, :T], f32),
        "x_own": np.ascontiguousarray(x_np[b, own0:own0 + n_own], f32),
        "wq": img(np.ascontiguousarray(Wq[cs:cs + 512, :].T)),
        "wk": img(np.ascontiguousarray(Wk[cs:cs + 512, :].T)),
        "wv": img(np.ascontiguousarray(Wv[cs:cs + 512, :].T)),
        "wos": img(np.ascontiguousarray(Wo.T[cs:cs + 512, :]), nch=4),
        "wrsg": img(np.ascontiguousarray(np.concatenate([Wr.T, Wsg.T], axis=1))),
        "wsgu": np.ascontiguousarray(wsgu_t),
        "wsd": np.ascontiguousarray(wsd_t),
        "weg": np.ascontiguousarray(weg_t),
        "wed": np.ascontiguousarray(wed_t),
        "ident": np.eye(128, dtype=f32),
        "ones_mat": np.ones((128, 128), f32),
        "strip": strip,
    }
    return d


def kernel(L0_attn_x, L0_attn_Wq, L0_attn_Wk, L0_attn_Wv, L0_attn_Wo,
           L0_ffn_W_router, L0_ffn_W_sgate, L0_ffn_Ws_gate_up, L0_ffn_Ws_down,
           L0_ffn_We_gate_up, L0_ffn_We_down):
    T = 8192
    n_own = T // 4
    x = np.asarray(L0_attn_x, np.float32)
    args = (np.asarray(L0_attn_Wq, np.float32), np.asarray(L0_attn_Wk, np.float32),
            np.asarray(L0_attn_Wv, np.float32), np.asarray(L0_attn_Wo, np.float32),
            np.asarray(L0_ffn_W_router, np.float32), np.asarray(L0_ffn_W_sgate, np.float32),
            np.asarray(L0_ffn_Ws_gate_up, np.float32), np.asarray(L0_ffn_Ws_down, np.float32),
            np.asarray(L0_ffn_We_gate_up, np.float32), np.asarray(L0_ffn_We_down, np.float32))
    in_maps = [_prep_core_inputs(c, x, *args, T) for c in range(8)]
    nc = build_nc(T=T)
    res = run_bass_kernel_spmd(nc, in_maps, core_ids=list(range(8)))
    out = np.empty((2, T, D), np.float32)
    for c in range(8):
        b = c // 4
        own0 = (c % 4) * n_own
        out[b, own0:own0 + n_own] = res.results[c]["out"]
    return out


# version 07
# speedup vs baseline: 1.0653x; 1.0653x over previous
import numpy as np
from contextlib import ExitStack

import concourse.bass as bass
import concourse.mybir as mybir
from concourse.bass_utils import run_bass_kernel_spmd

F32 = mybir.dt.float32
BF16 = mybir.dt.bfloat16
AF = mybir.ActivationFunctionType
ALU = mybir.AluOpType
AX = mybir.AxisListType

ENGS = ("sync", "scalar", "vector", "tensor", "gpsimd")

D = 2048
HD = 128
NCH = D // 128
NHL = 4
ISH = 5632
NICS = ISH // 128
IMOE = 1408
NICM = IMOE // 128
NEXP = 8
NICS_C = 44
NICM_C = 11
JBS_C = 256
NJBS_C = 2048 // JBS_C
JBM_C = 512
NJBM_C = 2048 // JBM_C
SCALE = float(np.float32(1.0) / np.float32(np.sqrt(np.float32(128.0))))
NEGBIG = -30000.0
EPS = 1e-5

SEMS = ["vv", "ss", "pp", "cc", "sc0",
        "sx", "sxg", "gw", "gww", "swk0", "swk1", "swq0", "swq1",
        "skh", "sqh", "szs",
        "srs", "srsb", "sxo", "sxob", "sx1", "sx1b",
        "sr10", "sr11", "sx20", "sx21", "sr2", "sout",
        "gwg0", "gwg1", "gwu0", "gwu1", "gwd0", "gwd1"]


def _s3(sb, c):
    return sb.ap().rearrange("p (c n) -> p c n", c=c)


class K:

    def __init__(self, nc):
        self.nc = nc
        self.cnt = {s: 0 for s in SEMS}
        self.lastw = {e: {} for e in ENGS}
        self.sems = {}
        self.ops = {e: [] for e in ENGS}
        self.pending_barrier = None
        self.barrier_done = set()

    def _op(self, eng, fn, waits, inc, k):
        wl = []
        if self.pending_barrier is not None and eng not in self.barrier_done:
            waits = self.pending_barrier + list(waits)
            self.barrier_done.add(eng)
        lw = self.lastw[eng]
        for s, c in waits:
            if c > 0 and lw.get(s, 0) < c:
                wl.append((s, c))
                lw[s] = c
        self.ops[eng].append((fn, wl, inc, k))
        if inc is None:
            return None
        self.cnt[inc] += k
        return self.cnt[inc]

    def v(self, fn, waits=()):
        return self._op("vector", fn, list(waits) + [("vv", self.cnt["vv"])], "vv", 1)

    def s(self, fn, waits=()):
        return self._op("scalar", fn, list(waits) + [("ss", self.cnt["ss"])], "ss", 1)

    def t(self, fn, waits=(), mark=False):
        return self._op("tensor", fn, waits, "pp" if mark else None, 1)

    def dma(self, eng, sem, fn, waits=()):
        return self._op(eng, fn, waits, sem, 16)

    def coll(self, fn, waits=()):
        return self._op("gpsimd", fn, waits, "cc", 1)

    def snap(self, extra=()):
        out = [(s, self.cnt[s]) for s in ("vv", "ss", "pp")]
        for s in extra:
            out.append((s, self.cnt[s]))
        return out

    def flush(self, barrier=True):
        nc = self.nc
        ops = self.ops
        sems = self.sems
        with nc.Block() as blk:
            for e in ENGS:
                lst = ops[e]
                if not lst:
                    continue

                def runner(eng, lst=lst):
                    for fn, wl, inc, k in lst:
                        for s, c in wl:
                            eng.wait_ge(sems[s], c)
                        inst = fn(eng)
                        if inc is not None:
                            inst.then_inc(sems[inc], k)

                getattr(blk, e)(runner)
        self.ops = {e: [] for e in ENGS}
        if barrier:
            self.pending_barrier = [(s, self.cnt[s]) for s in SEMS if self.cnt[s] > 0]
            self.barrier_done = set()


def emit_ln(kb, xt, tmp, out_ap, x_waits, d_inv, norm_waits=()):
    m_sum = kb.v(lambda e, a=tmp["s1"], b=xt: e.reduce_sum(a[:, :], b, axis=AX.X),
                 waits=x_waits)
    m_sq = kb.s(lambda e, o=tmp["sq"], i=xt, a=tmp["s2"]:
                e.activation(o[:, :], i, AF.Square, accum_out=a[:, :]),
                waits=x_waits)
    kb.v(lambda e, o=tmp["mean"], i=tmp["s1"]: e.tensor_scalar_mul(o[:, :], i[:, :], d_inv))
    kb.v(lambda e, o=tmp["m2"], i=tmp["mean"]: e.tensor_mul(o[:, :], i[:, :], i[:, :]))
    kb.v(lambda e, o=tmp["var"], i=tmp["s2"], j=tmp["m2"]:
         e.scalar_tensor_tensor(o[:, :], i[:, :], d_inv, j[:, :], ALU.mult, ALU.subtract),
         waits=[("ss", m_sq)])
    m_var = kb.cnt["vv"]
    m_sqrt = kb.s(lambda e, o=tmp["rstd0"], i=tmp["var"], b=kb.eps:
                  e.activation(o[:, :], i[:, :], AF.Sqrt, bias=b[:, 0:1]),
                  waits=[("vv", m_var)])
    kb.v(lambda e, o=tmp["rstd"], i=tmp["rstd0"]: e.reciprocal(o[:, :], i[:, :]),
         waits=[("ss", m_sqrt)])
    m_norm = kb.v(lambda e, o=out_ap, i=xt, m=tmp["mean"], r=tmp["rstd"]:
                  e.tensor_scalar(o, i, m[:, 0:1], r[:, 0:1], ALU.subtract, ALU.mult),
                  waits=norm_waits)
    return m_norm


def build_nc(T=8192, detect_race=True):
    NSUP = T // 512
    NQB = T // 512
    n_own = T // 4
    NTS = max(1, n_own // 512)
    HF = n_own // 2
    SUPF = min(512, HF)
    NTQ = HF // SUPF
    NTCH = HF // 128
    JBS = JBS_C
    NJBS = NJBS_C
    JBM = JBM_C
    NJBM = NJBM_C
    NOC = n_own // 128
    d_inv = 1.0 / D

    nc = bass.Bass("TRN2", target_bir_lowering=False, debug=False,
                   detect_race_conditions=detect_race)

    x_in = nc.dram_tensor("x", [T, D], F32, kind="ExternalInput").ap()
    xo_in = nc.dram_tensor("x_own", [n_own, D], F32, kind="ExternalInput").ap()
    wq_in = nc.dram_tensor("wq", [128, NCH * 512], F32, kind="ExternalInput").ap()
    wk_in = nc.dram_tensor("wk", [128, NCH * 512], F32, kind="ExternalInput").ap()
    wv_in = nc.dram_tensor("wv", [128, NCH * 512], F32, kind="ExternalInput").ap()
    wos_in = nc.dram_tensor("wos", [128, NCH * D], F32, kind="ExternalInput").ap()
    wrsg_in = nc.dram_tensor("wrsg", [128, NCH * 9], F32, kind="ExternalInput").ap()
    wsgu_in = nc.dram_tensor("wsgu", [2 * NICS * 128, NCH * 128], F32, kind="ExternalInput").ap()
    wsd_in = nc.dram_tensor("wsd", [NJBS_C * 128, NICS * JBS_C], F32, kind="ExternalInput").ap()
    weg_in = nc.dram_tensor("weg", [NEXP * 2 * NICM * 128, NCH * 128], F32, kind="ExternalInput").ap()
    wed_in = nc.dram_tensor("wed", [NEXP * NJBM_C * 128, NICM * JBM_C], F32, kind="ExternalInput").ap()
    ident_in = nc.dram_tensor("ident", [128, 128], F32, kind="ExternalInput").ap()
    onesm_in = nc.dram_tensor("ones_mat", [128, 128], F32, kind="ExternalInput").ap()
    strip_in = nc.dram_tensor("strip", [128, 896], F32, kind="ExternalInput").ap()
    oh_in = nc.dram_tensor("oh", [128, 4], F32, kind="ExternalInput").ap()
    out_t = nc.dram_tensor("out", [n_own, D], F32, kind="ExternalOutput").ap()

    k_scr = nc.dram_tensor("k_scr", [512, T], BF16).ap()
    q_scr = nc.dram_tensor("q_scr", [512, T], BF16).ap()
    z_rs = nc.dram_tensor("z_rs", [NHL * 2 * 2048, 1024], BF16).ap()
    zg_scr = nc.dram_tensor("zg_scr", [NHL * 2 * 512, 1024], BF16).ap()
    x1_scr = nc.dram_tensor("x1_scr", [n_own, D], F32).ap()
    x2_scr = nc.dram_tensor("x2_scr", [n_own, D], F32).ap()

    kb = K(nc)

    with ExitStack() as outer:
        E = outer.enter_context
        for s in SEMS:
            kb.sems[s] = E(nc.semaphore(s))

        ident = E(nc.sbuf_tensor("ident_bf", [128, 128], BF16))
        wrsg_f = E(nc.sbuf_tensor("wrsg_f", [128, NCH * 9], F32))
        wrhi = E(nc.sbuf_tensor("wrhi", [128, NCH * 9], BF16))
        wrhi32 = E(nc.sbuf_tensor("wrhi32", [128, NCH * 9], F32))
        wrlo = E(nc.sbuf_tensor("wrlo", [128, NCH * 9], BF16))
        w_all = E(nc.sbuf_tensor("w_all", [128, NOC * 8], F32))
        eps_sb = E(nc.sbuf_tensor("eps_sb", [128, 1], F32))
        kb.eps = eps_sb
        sg_all = E(nc.sbuf_tensor("sg_all", [128, NOC], F32))
        oh_sb = E(nc.sbuf_tensor("oh_sb", [128, 4], F32))

        c_ident = kb.dma("gpsimd", "gw", lambda e: e.dma_start(out=ident[:, :], in_=ident_in))
        kb.dma("sync", "sc0", lambda e: e.dma_start(out=oh_sb[:, :], in_=oh_in))
        c_wrsg = kb.dma("sync", "sc0", lambda e: e.dma_start(
            out=wrsg_f[:, :], in_=wrsg_in))
        kb.v(lambda e, o=eps_sb: e.memset(o[:, :], EPS))
        kb.v(lambda e: e.tensor_copy(wrhi[:, :], wrsg_f[:, :]), waits=[("sc0", c_wrsg)])
        kb.v(lambda e: e.tensor_copy(wrhi32[:, :], wrhi[:, :]))
        kb.v(lambda e: e.tensor_sub(wrlo[:, :], wrsg_f[:, :], wrhi32[:, :]))

        v_stack = ExitStack()
        v_sb = v_stack.enter_context(nc.sbuf_tensor("v_sb", [128, NHL * T], BF16))
        wos_stack = ExitStack()

        with ExitStack() as sA:
            EA = sA.enter_context
            wqs = EA(nc.sbuf_tensor("wqs", [128, NCH * 512], BF16))
            wks = EA(nc.sbuf_tensor("wks", [128, NCH * 512], BF16))
            wvs = EA(nc.sbuf_tensor("wvs", [128, NCH * 512], BF16))
            x_t = [EA(nc.sbuf_tensor(f"x_t{i}", [128, D], F32)) for i in range(2)]
            xln_t = [EA(nc.sbuf_tensor(f"xln_t{i}", [128, D], BF16)) for i in range(2)]
            xfm = EA(nc.sbuf_tensor("xfm", [128, NCH * 512], BF16))
            sq_sb = EA(nc.sbuf_tensor("sq_sb", [128, D], F32))
            kst = [EA(nc.sbuf_tensor(f"kst{i}", [128, 512], BF16)) for i in range(2)]
            qst = [EA(nc.sbuf_tensor(f"qst{i}", [128, 512], BF16)) for i in range(2)]
            tmpA = [{n: EA(nc.sbuf_tensor(f"lnA{p}_{n}", [128, 1], F32))
                     for n in ("s1", "s2", "mean", "m2", "var", "rstd0", "rstd")}
                    for p in range(2)]
            for p in range(2):
                tmpA[p]["sq"] = sq_sb
            ps_t = [EA(nc.psum_tensor(f"ps_t{i}", [128, 128], BF16)) for i in range(2)]
            ps_kq = [EA(nc.psum_tensor(f"ps_kq{i}", [128, 512], F32)) for i in range(2)]
            ps_v = [EA(nc.psum_tensor(f"ps_v{i}", [128, 512], F32)) for i in range(2)]

            c_w3 = 0
            for w_sb, w_src in ((wqs, wq_in), (wks, wk_in), (wvs, wv_in)):
                c_w3 = kb.dma("gpsimd", "gww",
                              lambda e, o=w_sb, i=w_src: e.dma_start(out=o[:, :], in_=i))

            ln_marks = {-2: (0, 0), -1: (0, 0)}
            tp_done_t = {-2: 0, -1: 0}
            tposev_mark = [0, 0]
            kq_war = [0, 0]
            v_war = [0, 0]
            kst_war = [0, 0]
            qst_war = [0, 0]
            tr_cnt = 0
            XQ = [("sync", "sx"), ("scalar", "sxg")]

            for st_i in range(NSUP):
                t0 = st_i * 512
                for a in range(4):
                    t = st_i * 4 + a
                    par = t % 2
                    vvf, ssf = ln_marks[t - 2]
                    qn, sn = XQ[par]
                    c_x = kb.dma(qn, sn,
                                 lambda e, o=x_t[par], i=x_in[t * 128:(t + 1) * 128, :]:
                                 e.dma_start(out=o[:, :], in_=i),
                                 waits=[("vv", vvf), ("ss", ssf)])
                    m_n = emit_ln(kb, x_t[par][:, :], tmpA[par], xln_t[par][:, :],
                                  [(sn, c_x)], d_inv,
                                  norm_waits=[("pp", tp_done_t[t - 2])])
                    ln_marks[t] = (kb.cnt["vv"], kb.cnt["ss"])
                    for c in range(NCH):
                        ptp = tr_cnt % 2
                        tr_cnt += 1
                        m_t = kb.t(lambda e, o=ps_t[ptp],
                                   i=xln_t[par][:, c * 128:(c + 1) * 128], idn=ident:
                                   e.transpose(o[:, :], i, idn[:, :]),
                                   waits=[("vv", m_n), ("ss", tposev_mark[ptp]),
                                          ("gw", c_ident)],
                                   mark=True)
                        kb.s(lambda e, o=xfm[:, c * 512 + a * 128: c * 512 + (a + 1) * 128],
                             i=ps_t[ptp]: e.copy(o, i[:, :]),
                             waits=[("pp", m_t)])
                        tposev_mark[ptp] = kb.cnt["ss"]
                    tp_done_t[t] = kb.cnt["pp"]
                m_evac_all = kb.cnt["ss"]

                for w_sb, stg, scr, sems2, war, stq in (
                        (wks, kst, k_scr, ("swk0", "swk1"), kst_war, "sync"),
                        (wqs, qst, q_scr, ("swq0", "swq1"), qst_war, "scalar")):
                    for j in range(4):
                        par = j % 2
                        for c in range(NCH):
                            kb.t(lambda e, o=ps_kq[par],
                                 l=w_sb[:, c * 512 + j * 128: c * 512 + (j + 1) * 128],
                                 r=xfm[:, c * 512:(c + 1) * 512],
                                 st=(c == 0), sp=(c == NCH - 1):
                                 e.matmul(o[:, :], l, r, start=st, stop=sp),
                                 waits=[("ss", m_evac_all), ("gww", c_w3),
                                        ("vv", kq_war[par])],
                                 mark=(c == NCH - 1))
                        m_mm = kb.cnt["pp"]
                        kb.v(lambda e, o=stg[par], i=ps_kq[par]: e.tensor_copy(o[:, :], i[:, :]),
                             waits=[("pp", m_mm), (sems2[par], war[par])])
                        m_ev = kb.cnt["vv"]
                        kq_war[par] = m_ev
                        war[par] = kb.dma(stq, sems2[par],
                                          lambda e, o=scr[j * 128:(j + 1) * 128, t0:t0 + 512],
                                          i=stg[par]: e.dma_start(out=o, in_=i[:, :]),
                                          waits=[("vv", m_ev)])
                for a in range(4):
                    par = a % 2
                    for c in range(NCH):
                        kb.t(lambda e, o=ps_v[par],
                             l=xfm[:, c * 512 + a * 128: c * 512 + (a + 1) * 128],
                             r=wvs[:, c * 512:(c + 1) * 512],
                             st=(c == 0), sp=(c == NCH - 1):
                             e.matmul(o[:, :], l, r, start=st, stop=sp),
                             waits=[("ss", m_evac_all), ("gww", c_w3), ("vv", v_war[par])],
                             mark=(c == NCH - 1))
                    m_mm = kb.cnt["pp"]
                    kc = t0 // 128 + a
                    for h in range(NHL):
                        kb.v(lambda e, o=v_sb[:, h * T + kc * 128:h * T + (kc + 1) * 128],
                             i=ps_v[par][:, h * 128:(h + 1) * 128]: e.tensor_copy(o, i),
                             waits=[("pp", m_mm)])
                    v_war[par] = kb.cnt["vv"]
            kb.flush()

        EW = wos_stack.enter_context
        wos_sb = EW(nc.sbuf_tensor("wos_sb", [128, NCH * D], BF16, side="right"))

        with ExitStack() as sB:
            EB = sB.enter_context
            NR = 4
            kh = EB(nc.sbuf_tensor("kh", [128, T], BF16))
            qh = EB(nc.sbuf_tensor("qh", [128, T], BF16))
            onesm = EB(nc.sbuf_tensor("onesm_bf", [128, 128], BF16))
            strip = EB(nc.sbuf_tensor("strip_f", [128, 896], F32))
            p_ring = [EB(nc.sbuf_tensor(f"p_ring{i}", [128, 1024], BF16)) for i in range(NR)]
            rbc_sb = [EB(nc.sbuf_tensor(f"rbc_sb{i}", [128, 512], F32)) for i in range(2)]
            zring = [EB(nc.sbuf_tensor(f"zring{i}", [128, 4 * 512], BF16)) for i in range(2)]
            ps_s = [EB(nc.psum_tensor(f"ps_s{i}", [128, 1024], F32)) for i in range(2)]
            ps_o = [EB(nc.psum_tensor(f"ps_o{i}", [128, 512], F32)) for i in range(2)]
            ps_lb = [EB(nc.psum_tensor(f"ps_lb{i}", [128, 512], F32)) for i in range(2)]

            c_onesm = kb.dma("gpsimd", "gw", lambda e: e.dma_start(
                out=onesm[:, :], in_=onesm_in))
            c_strip = kb.dma("sync", "sc0", lambda e: e.dma_start(
                out=strip[:, :], in_=strip_in))
            c_wos = kb.dma("gpsimd", "gw", lambda e: e.dma_start(
                out=wos_sb[:, :], in_=wos_in))

            exp_mark = {0: 0, 1: 0}
            ring_free = {}
            o_war = [0, 0]
            lb_war = [0, 0]
            zst_mark = [0, 0]
            pg = 0
            c_kh = c_qh = 0

            def emit_avl(pend, opar, nk, h):
                slot, kc0, m_exp = pend
                for half in (0, 1):
                    kc = kc0 + half
                    kb.t(lambda e, o=ps_o[opar],
                         l=v_sb[:, h * T + kc * 128:h * T + (kc + 1) * 128],
                         r=p_ring[slot][:, half * 512:(half + 1) * 512],
                         st=(kc == 0), sp=(kc == nk - 1):
                         e.matmul(o[:, :], l, r, start=st, stop=sp),
                         waits=[("ss", m_exp)] + ([("vv", o_war[opar])] if kc == 0 else []),
                         mark=(kc == nk - 1))
                    kb.t(lambda e, o=ps_lb[opar], l=onesm,
                         r=p_ring[slot][:, half * 512:(half + 1) * 512],
                         st=(kc == 0), sp=(kc == nk - 1):
                         e.matmul(o[:, :], l[:, :], r, start=st, stop=sp),
                         waits=[("ss", m_exp), ("gw", c_onesm)] +
                               ([("vv", lb_war[opar])] if kc == 0 else []),
                         mark=(half == 1))
                ring_free[slot] = kb.cnt["pp"]

            for h in range(NHL):
                c_kh = kb.dma("sync", "skh",
                              lambda e, o=kh, i=k_scr[h * 128:(h + 1) * 128, :]:
                              e.dma_start(out=o[:, :], in_=i),
                              waits=kb.snap(("swk0", "swk1")))
                c_qh = kb.dma("scalar", "sqh",
                              lambda e, o=qh, i=q_scr[h * 128:(h + 1) * 128, :]:
                              e.dma_start(out=o[:, :], in_=i),
                              waits=kb.snap(("swq0", "swq1")))
                for qb in range(NQB):
                    nk = 4 * (qb + 1)
                    opar = qb % 2
                    pend = None
                    for j in range(nk // 2):
                        kc0 = 2 * j
                        spar = pg % 2
                        slot = pg % NR
                        pg += 1
                        m_s = 0
                        for half in (0, 1):
                            kc = kc0 + half
                            m_s = kb.t(lambda e, o=ps_s[spar][:, half * 512:(half + 1) * 512],
                                       l=kh[:, kc * 128:(kc + 1) * 128],
                                       r=qh[:, qb * 512:(qb + 1) * 512]:
                                       e.matmul(o, l, r, start=True, stop=True),
                                       waits=[("skh", c_kh), ("sqh", c_qh),
                                              ("ss", exp_mark[spar])],
                                       mark=True)
                        mask_w = []
                        for half in (0, 1):
                            kc = kc0 + half
                            if kc >= 4 * qb:
                                s_off = 384 - 128 * (kc - 4 * qb)
                                kb.v(lambda e, o=ps_s[spar][:, half * 512:(half + 1) * 512],
                                     m=strip[:, s_off:s_off + 512]:
                                     e.tensor_tensor(o, o, m, ALU.add),
                                     waits=[("pp", m_s - 1 + half), ("sc0", c_strip)])
                                mask_w = [("vv", kb.cnt["vv"])]
                        if pend is not None:
                            emit_avl(pend, opar, nk, h)
                        m_exp = kb.s(lambda e, o=p_ring[slot], i=ps_s[spar]:
                                     e.activation(o[:, :], i[:, :], AF.Exp, scale=SCALE),
                                     waits=[("pp", m_s), ("pp", ring_free.get(slot, 0))] + mask_w)
                        exp_mark[spar] = m_exp
                        pend = (slot, kc0, m_exp)
                    emit_avl(pend, opar, nk, h)
                    m_lend = kb.cnt["pp"]
                    kb.v(lambda e, o=rbc_sb[opar], i=ps_lb[opar]:
                         e.reciprocal(o[:, :], i[:, :]),
                         waits=[("pp", m_lend)])
                    lb_war[opar] = kb.cnt["vv"]
                    rp = (h * NQB + qb) % 2
                    for slot4 in range(4):
                        kb.v(lambda e, o=zring[rp][:, slot4 * 512:(slot4 + 1) * 512],
                             i=ps_o[opar], s1=oh_sb[:, slot4:slot4 + 1], j=rbc_sb[opar]:
                             e.scalar_tensor_tensor(o, i[:, :], s1, j[:, :],
                                                    ALU.mult, ALU.mult),
                             waits=([("szs", zst_mark[rp])] if slot4 == 0 else []))
                    o_war[opar] = kb.cnt["vv"]
                    m_z4 = kb.cnt["vv"]
                    q4 = qb // 4
                    ch_q = (qb % 4) // 2
                    cwh = ((qb % 4) % 2) * 512
                    base = (h * 2 + ch_q) * 2048 + q4 * 512
                    zst_mark[rp] = kb.dma(
                        "sync", "szs",
                        lambda e,
                        o=z_rs[base:base + 512, cwh:cwh + 512].rearrange(
                            "(s p) n -> p s n", p=128),
                        i=zring[rp].ap().rearrange("p (s n) -> p s n", s=4):
                        e.dma_start(out=o, in_=i),
                        waits=[("vv", m_z4)])
                    if qb == 13 or qb == 15:
                        ch = 0 if qb == 13 else 1
                        kb.coll(lambda e, h=h, ch=ch: e.collective_compute(
                            "ReduceScatter", ALU.add,
                            replica_groups=[[0, 1, 2, 3], [4, 5, 6, 7]],
                            ins=[z_rs[(h * 2 + ch) * 2048:(h * 2 + ch + 1) * 2048, :]],
                            outs=[zg_scr[(h * 2 + ch) * 512:(h * 2 + ch + 1) * 512, :]]),
                            waits=[("szs", kb.cnt["szs"])])
            kb.flush()
        v_stack.close()

        xfm2 = E(nc.sbuf_tensor("xfm2", [128, NCH * n_own], BF16))

        up_stacks = {0: ExitStack(), 1: ExitStack()}

        def alloc_up(hf):
            EU = up_stacks[hf].enter_context
            return dict(
                wg=[EU(nc.sbuf_tensor(f"wg{i}_{hf}", [128, NCH * 128], BF16, side="right"))
                    for i in range(2)],
                wu=[EU(nc.sbuf_tensor(f"wu{i}_{hf}", [128, NCH * 128], BF16, side="right"))
                    for i in range(2)],
                silu=[EU(nc.sbuf_tensor(f"silu{i}_{hf}", [128, SUPF], F32, side="right"))
                      for i in range(2)],
                ps_g=[EU(nc.psum_tensor(f"ps_g{i}_{hf}", [128, SUPF], F32)) for i in range(2)],
                ps_u=[EU(nc.psum_tensor(f"ps_u{i}_{hf}", [128, SUPF], F32)) for i in range(2)])

        def emit_wload(bufs, ic, waits):
            ring = ic % 2
            c_g = kb.dma("gpsimd", ("gwg0", "gwg1")[ring],
                         lambda e, o=bufs["wg"][ring],
                         i=wsgu_in[ic * 128:(ic + 1) * 128, :]:
                         e.dma_start(out=o[:, :], in_=i),
                         waits=waits)
            c_u = kb.dma("gpsimd", ("gwu0", "gwu1")[ring],
                         lambda e, o=bufs["wu"][ring],
                         i=wsgu_in[(NICS + ic) * 128:(NICS + ic + 1) * 128, :]:
                         e.dma_start(out=o[:, :], in_=i),
                         waits=waits)
            return (c_g, c_u)

        def init_waits():
            return [(s, kb.cnt[s]) for s in
                    ("vv", "ss", "pp", "srs", "srsb", "sxo", "sxob",
                     "sx1", "sx1b", "sr10", "sr11", "sx20", "sx21")]

        st_ss = {}
        x1_cnt = {}

        with ExitStack() as sC:
            EC = sC.enter_context
            zg_sb = [EC(nc.sbuf_tensor(f"zg_sb{i}", [128, NCH * 128], BF16)) for i in range(2)]
            xo_sb = EC(nc.sbuf_tensor("xo_sb", [128, D], F32))
            x1_sb = EC(nc.sbuf_tensor("x1_sb", [128, D], F32))
            xhi = EC(nc.sbuf_tensor("xhi", [128, D], BF16))
            xhi32 = EC(nc.sbuf_tensor("xhi32", [128, D], F32))
            xlo = EC(nc.sbuf_tensor("xlo", [128, D], BF16))
            xlo_fm = EC(nc.sbuf_tensor("xlo_fm", [128, D], BF16))
            pr8 = EC(nc.sbuf_tensor("pr8", [128, 8], F32))
            eq8 = EC(nc.sbuf_tensor("eq8", [128, 8], F32))
            msk8 = EC(nc.sbuf_tensor("msk8", [128, 8], F32))
            mx1 = EC(nc.sbuf_tensor("mx1", [128, 1], F32))
            nmax = EC(nc.sbuf_tensor("nmax", [128, 1], F32))
            se8 = EC(nc.sbuf_tensor("se8", [128, 1], F32))
            rse = EC(nc.sbuf_tensor("rse", [128, 1], F32))
            m2r = EC(nc.sbuf_tensor("m2r", [128, 1], F32))
            tmpc = {n: EC(nc.sbuf_tensor(f"lnc_{n}", [128, 1], F32))
                    for n in ("s1", "s2", "mean", "m2", "var", "rstd0", "rstd")}
            xln32 = xo_sb
            tmpc["sq"] = xhi32
            ps_j = [EC(nc.psum_tensor(f"ps_j{i}", [128, 512], F32)) for i in range(2)]
            ps_t2 = [EC(nc.psum_tensor(f"ps_t2{i}", [128, 128], BF16)) for i in range(2)]
            ps_lg = EC(nc.psum_tensor("ps_lg", [128, 9], F32))

            tposev_mark2 = [0, 0]
            tr2 = 0
            lg_w_vv, lg_w_ss = 0, 0
            tp_mark = 0
            hilo_m = {}
            tev_m = {}
            mx1_m = {}
            zg_read = [0, 0]
            psj_war = [0, 0]
            xo_read = 0
            x1_prev = [("sx1", 0), ("sx1b", 0)]
            CQ = [("sync", "srs", "sxo", "sx1"), ("scalar", "srsb", "sxob", "sx1b")]

            def emit_tr_router(t):
                nonlocal tr2, lg_w_vv, lg_w_ss, tp_mark
                trow = t * 128
                m_hilo = hilo_m[t]
                for c in range(NCH):
                    for src, dst in ((xhi[:, c * 128:(c + 1) * 128],
                                      xfm2[:, c * n_own + trow:c * n_own + trow + 128]),
                                     (xlo[:, c * 128:(c + 1) * 128],
                                      xlo_fm[:, c * 128:(c + 1) * 128])):
                        p2 = tr2 % 2
                        tr2 += 1
                        m_t = kb.t(lambda e, o=ps_t2[p2],
                                   i=src, idn=ident:
                                   e.transpose(o[:, :], i, idn[:, :]),
                                   waits=[("vv", m_hilo), ("ss", tposev_mark2[p2])],
                                   mark=True)
                        kb.s(lambda e, o=dst, i=ps_t2[p2]:
                             e.copy(o, i[:, :]),
                             waits=[("pp", m_t)])
                        tposev_mark2[p2] = kb.cnt["ss"]
                tp_mark = kb.cnt["pp"]
                m_tev = kb.cnt["ss"]
                tev_m[t] = m_tev
                n_mm = 0
                for xsrc, wsrc in ((xfm2, wrhi), (xfm2, wrlo), (xlo_fm, wrhi)):
                    for c in range(NCH):
                        if xsrc is xfm2:
                            l_ap = xfm2[:, c * n_own + trow:c * n_own + trow + 128]
                        else:
                            l_ap = xlo_fm[:, c * 128:(c + 1) * 128]
                        st = (n_mm == 0)
                        sp = (n_mm == 3 * NCH - 1)
                        kb.t(lambda e, o=ps_lg, l=l_ap, r=wsrc[:, c * 9:(c + 1) * 9],
                             st=st, sp=sp: e.matmul(o[:, :], l, r, start=st, stop=sp),
                             waits=[("ss", m_tev), ("vv", lg_w_vv), ("ss", lg_w_ss)],
                             mark=sp)
                        n_mm += 1
                m_lg = kb.cnt["pp"]
                kb.v(lambda e, o=mx1, i=ps_lg: e.reduce_max(o[:, :], i[:, 0:8], axis=AX.X),
                     waits=[("pp", m_lg)])
                kb.v(lambda e, o=nmax, i=mx1: e.tensor_scalar_mul(o[:, :], i[:, :], -1.0))
                m_nm = kb.cnt["vv"]
                m_pe = kb.s(lambda e, o=pr8, i=ps_lg, b=nmax, a=se8:
                            e.activation(o[:, :], i[:, 0:8], AF.Exp,
                                         bias=b[:, 0:1], accum_out=a[:, :]),
                            waits=[("vv", m_nm), ("pp", m_lg)])
                kb.v(lambda e, o=rse, i=se8: e.reciprocal(o[:, :], i[:, :]),
                     waits=[("ss", m_pe)])
                kb.v(lambda e, o=pr8, i=pr8, r=rse:
                     e.tensor_scalar(o[:, :], i[:, :], r[:, 0:1], None, ALU.mult))
                kb.v(lambda e, o=mx1, i=pr8: e.reduce_max(o[:, :], i[:, :], axis=AX.X))
                kb.v(lambda e, o=eq8, i=pr8, m=mx1:
                     e.tensor_scalar(o[:, :], i[:, :], m[:, 0:1], None, ALU.is_equal))
                kb.v(lambda e, o=eq8, i=pr8:
                     e.scalar_tensor_tensor(o[:, :], o[:, :], -1e30, i[:, :], ALU.mult, ALU.add))
                kb.v(lambda e, o=m2r, i=eq8: e.reduce_max(o[:, :], i[:, :], axis=AX.X))
                kb.v(lambda e, o=msk8, i=pr8, m=m2r:
                     e.tensor_scalar(o[:, :], i[:, :], m[:, 0:1], None, ALU.is_ge))
                kb.v(lambda e, o=w_all[:, t * 8:(t + 1) * 8], i=pr8, m=msk8:
                     e.tensor_mul(o, i[:, :], m[:, :]))
                lg_w_vv = kb.cnt["vv"]
                lg_w_ss = kb.s(lambda e, o=sg_all[:, t:t + 1], i=ps_lg:
                               e.activation(o, i[:, 8:9], AF.Sigmoid),
                               waits=[("pp", m_lg)])
                if t % 4 == 3:
                    st_ss[t // 4] = kb.cnt["ss"]

            for t in range(NOC):
                par = t % 2
                trow = t * 128
                ch_t = t // 8
                col = (t * 128) % 1024
                qn, s_zg, s_xo, s_x1 = CQ[par]
                c_zg = 0
                for hh in range(4):
                    c_zg = kb.dma(qn, s_zg,
                                  lambda e,
                                  o=zg_sb[par].ap().rearrange("p (h r n) -> p (h r) n",
                                                              h=4, r=4)[
                                      :, hh * 4:(hh + 1) * 4, :],
                                  i=zg_scr[(hh * 2 + ch_t) * 512:(hh * 2 + ch_t + 1) * 512,
                                           col:col + 128].rearrange(
                                      "(r p) n -> p r n", p=128):
                                  e.dma_start(out=o, in_=i),
                                  waits=[("cc", 7 + ch_t), ("pp", zg_read[par])])
                c_xo = kb.dma(qn, s_xo,
                              lambda e, o=xo_sb, i=xo_in[trow:trow + 128, :]:
                              e.dma_start(out=o[:, :], in_=i),
                              waits=[("vv", xo_read)])
                ev = []
                for jh in range(2):
                    for g in range(16):
                        cb = (g % 4) * 4 + g // 4
                        for j2 in range(2):
                            j = jh * 2 + j2
                            kb.t(lambda e, o=ps_j[j2],
                                 l=zg_sb[par][:, cb * 128:(cb + 1) * 128],
                                 r=wos_sb[:, g * D + j * 512:g * D + (j + 1) * 512],
                                 st=(g == 0), sp=(g == 15):
                                 e.matmul(o[:, :], l, r, start=st, stop=sp),
                                 waits=[(s_zg, c_zg), ("gw", c_wos)] +
                                       ([("vv", psj_war[j2])] if g == 0 else []),
                                 mark=(g == 15))
                    m_j = kb.cnt["pp"]
                    for j2 in range(2):
                        j = jh * 2 + j2
                        kb.v(lambda e, o=x1_sb[:, j * 512:(j + 1) * 512],
                             i=ps_j[j2], j3=xo_sb[:, j * 512:(j + 1) * 512]:
                             e.scalar_tensor_tensor(o, i[:, :], 1.0, j3,
                                                    ALU.mult, ALU.add),
                             waits=[("pp", m_j - 1 + j2), (s_xo, c_xo),
                                    x1_prev[par]])
                        psj_war[j2] = kb.cnt["vv"]
                zg_read[par] = kb.cnt["pp"]
                m_x1 = kb.cnt["vv"]
                c_x1 = kb.dma(qn, s_x1,
                              lambda e, o=x1_scr[trow:trow + 128, :], i=x1_sb:
                              e.dma_start(out=o, in_=i[:, :]),
                              waits=[("vv", m_x1)])
                x1_prev[par] = (s_x1, c_x1)
                if t > 0:
                    emit_tr_router(t - 1)
                emit_ln(kb, x1_sb[:, :], tmpc, xln32[:, :], [("vv", m_x1)], d_inv)
                kb.v(lambda e, o=xhi, i=xln32: e.tensor_copy(o[:, :], i[:, :]),
                     waits=[("pp", tp_mark)])
                kb.v(lambda e, o=xhi32, i=xhi: e.tensor_copy(o[:, :], i[:, :]))
                kb.v(lambda e, o=xlo, i=xln32, j=xhi32: e.tensor_sub(o[:, :], i[:, :], j[:, :]))
                hilo_m[t] = kb.cnt["vv"]
                xo_read = hilo_m[t]
                if t % 4 == 3:
                    x1_cnt[t // 4] = (kb.cnt["sx1"], kb.cnt["sx1b"])
            emit_tr_router(NOC - 1)
            kb.flush(barrier=False)
        wos_stack.close()
        c_end = init_waits()

        with ExitStack() as sE:
            EE = sE.enter_context
            hsh = EE(nc.sbuf_tensor("hsh", [128, NICS * HF], BF16))
            for hf in range(2):
                hbase = hf * HF
                half_pp = kb.cnt["pp"]
                bufs = alloc_up(hf)
                wg_sb, wu_sb, silu_sb = bufs["wg"], bufs["wu"], bufs["silu"]
                ps_g, ps_u = bufs["ps_g"], bufs["ps_u"]
                up1_init = c_end if hf == 0 else init_waits()
                v0u = kb.cnt["vv"]
                p0u = kb.cnt["pp"]
                wld_war = [p0u, p0u]
                gu_war = [v0u, v0u]
                silu_war = [v0u, v0u]
                for ic in range(NICS):
                    ring = ic % 2
                    wload = up1_init if ic < 2 else [("pp", wld_war[ring])]
                    c_g, c_u = emit_wload(bufs, ic, wload)
                    for tq in range(NTQ):
                        tqb = hbase + tq * SUPF
                        par = tq % 2
                        for c in range(NCH):
                            kb.t(lambda e, o=ps_g[par],
                                 l=wg_sb[ring][:, c * 128:(c + 1) * 128],
                                 r=xfm2[:, c * n_own + tqb:c * n_own + tqb + SUPF],
                                 st=(c == 0), sp=(c == NCH - 1):
                                 e.matmul(o[:, :], l, r, start=st, stop=sp),
                                 waits=[(("gwg0", "gwg1")[ring], c_g), ("vv", gu_war[par]),
                                        ("ss", st_ss[hf * 2 + tq])],
                                 mark=(c == NCH - 1))
                        m_g = kb.cnt["pp"]
                        for c in range(NCH):
                            kb.t(lambda e, o=ps_u[par],
                                 l=wu_sb[ring][:, c * 128:(c + 1) * 128],
                                 r=xfm2[:, c * n_own + tqb:c * n_own + tqb + SUPF],
                                 st=(c == 0), sp=(c == NCH - 1):
                                 e.matmul(o[:, :], l, r, start=st, stop=sp),
                                 waits=[(("gwu0", "gwu1")[ring], c_u), ("vv", gu_war[par]),
                                        ("ss", st_ss[hf * 2 + tq])],
                                 mark=(c == NCH - 1))
                        m_u = kb.cnt["pp"]
                        if tq == NTQ - 1:
                            wld_war[ring] = m_u
                        m_si = kb.s(lambda e, o=silu_sb[par], i=ps_g[par]:
                                    e.activation(o[:, :], i[:, :], AF.Silu),
                                    waits=[("pp", m_g), ("vv", silu_war[par])])
                        kb.v(lambda e, o=hsh[:, ic * HF + tq * SUPF:ic * HF + (tq + 1) * SUPF],
                             i=ps_u[par], j=silu_sb[par]:
                             e.tensor_tensor(o, i[:, :], j[:, :], ALU.mult),
                             waits=[("pp", m_u), ("ss", m_si), ("pp", half_pp)])
                        gu_war[par] = kb.cnt["vv"]
                        silu_war[par] = gu_war[par]
                m_hsh = kb.cnt["vv"]
                up_stacks[hf].close()
                with ExitStack() as sDn:
                    ED = sDn.enter_context
                    ICQ = NICS // 4
                    wsdb = ED(nc.sbuf_tensor(f"wsdb_{hf}", [128, NICS * JBS], BF16))
                    x1s = [ED(nc.sbuf_tensor(f"x1s{i}_{hf}", [128, JBS], F32)) for i in range(2)]
                    x2s = [ED(nc.sbuf_tensor(f"x2s{i}_{hf}", [128, JBS], F32)) for i in range(2)]
                    ps_d = [ED(nc.psum_tensor(f"ps_d{i}_{hf}", [128, JBS], F32)) for i in range(2)]
                    dn_init = c_end if hf == 0 else up1_init
                    v0d = kb.cnt["vv"]
                    wd_war = 0
                    d_war = [v0d, v0d]
                    x1_war = [v0d, v0d]
                    x2_war = [kb.cnt["sx20"], kb.cnt["sx21"]]
                    for jb in range(NJBS):
                        wload = dn_init if jb == 0 else [("pp", wd_war)]
                        c_dq = []
                        for q in range(4):
                            c_dq.append(kb.dma(
                                "gpsimd", "gwd0",
                                lambda e, o=wsdb[:, q * ICQ * JBS:(q + 1) * ICQ * JBS],
                                i=wsd_in[jb * 128:(jb + 1) * 128,
                                         q * ICQ * JBS:(q + 1) * ICQ * JBS]:
                                e.dma_start(out=o, in_=i),
                                waits=wload if q == 0 else ()))
                        for tc in range(NTCH):
                            par = tc % 2
                            trow = hbase + tc * 128
                            tcg = trow // 128
                            c_x1 = kb.dma("scalar", ("sr10", "sr11")[par],
                                          lambda e, o=x1s[par],
                                          i=x1_scr[trow:trow + 128, jb * JBS:(jb + 1) * JBS]:
                                          e.dma_start(out=o[:, :], in_=i),
                                          waits=(dn_init if jb == 0 and tc < 2 else []) +
                                                [("vv", x1_war[par]),
                                                 ("sx1", x1_cnt[hf * 2 + 1][0]),
                                                 ("sx1b", x1_cnt[hf * 2 + 1][1])])
                            for ic in range(NICS):
                                kb.t(lambda e, o=ps_d[par],
                                     l=hsh[:, ic * HF + tc * 128:ic * HF + (tc + 1) * 128],
                                     r=wsdb[:, ic * JBS:(ic + 1) * JBS],
                                     st=(ic == 0), sp=(ic == NICS - 1):
                                     e.matmul(o[:, :], l, r, start=st, stop=sp),
                                     waits=[("gwd0", c_dq[min(ic // ICQ, 3)]),
                                            ("vv", m_hsh), ("vv", d_war[par])],
                                     mark=(ic == NICS - 1))
                            m_mm = kb.cnt["pp"]
                            if tc == NTCH - 1:
                                wd_war = m_mm
                            kb.v(lambda e, o=x2s[par], i=ps_d[par],
                                 s1=sg_all[:, tcg:tcg + 1], j=x1s[par]:
                                 e.scalar_tensor_tensor(o[:, :], i[:, :], s1, j[:, :],
                                                        ALU.mult, ALU.add),
                                 waits=[("pp", m_mm), (("sr10", "sr11")[par], c_x1),
                                        (("sx20", "sx21")[par], x2_war[par])])
                            d_war[par] = kb.cnt["vv"]
                            x1_war[par] = d_war[par]
                            x2_war[par] = kb.dma(
                                "sync", ("sx20", "sx21")[par],
                                lambda e, o=x2_scr[trow:trow + 128, jb * JBS:(jb + 1) * JBS],
                                i=x2s[par]: e.dma_start(out=o, in_=i[:, :]),
                                waits=[("vv", d_war[par])])
            kb.flush()

        with ExitStack() as sF:
            EF = sF.enter_context
            acc = EF(nc.sbuf_tensor("acc", [128, NTCH * D], F32))
            he = EF(nc.sbuf_tensor("he", [128, NICM * HF], BF16))
            wg_sb = [EF(nc.sbuf_tensor(f"mwg{i}", [128, NCH * 128], BF16)) for i in range(2)]
            wu_sb = [EF(nc.sbuf_tensor(f"mwu{i}", [128, NCH * 128], BF16)) for i in range(2)]
            silu_m = [EF(nc.sbuf_tensor(f"msilu{i}", [128, SUPF], F32)) for i in range(2)]
            wedb = [EF(nc.sbuf_tensor(f"wedb{i}", [128, NICM * JBM], BF16)) for i in range(2)]
            ps_g = [EF(nc.psum_tensor(f"mps_g{i}", [128, SUPF], F32)) for i in range(2)]
            ps_u = [EF(nc.psum_tensor(f"mps_u{i}", [128, SUPF], F32)) for i in range(2)]
            ps_m = [EF(nc.psum_tensor(f"mps_m{i}", [128, JBM], F32)) for i in range(2)]
            for hf in range(2):
                hbase = hf * HF
                accw = kb.snap(("sout",)) if hf else kb.snap()
                c_acc = 0
                for tc in range(NTCH):
                    c_acc = kb.dma("scalar", "sr2",
                                   lambda e, o=acc[:, tc * D:(tc + 1) * D],
                                   i=x2_scr[hbase + tc * 128:hbase + (tc + 1) * 128, :]:
                                   e.dma_start(out=o, in_=i),
                                   waits=accw)
                g0 = kb.cnt["pp"]
                v0 = kb.cnt["vv"]
                wld_war = [g0, g0]
                gu_war = [v0, v0]
                silu_war = [v0, v0]
                wd_war = [g0, g0]
                d_war = [v0, v0]
                he_war = g0
                for ex in range(NEXP):
                    for ic in range(NICM):
                        ring = ic % 2
                        first = (hf == 0 and ex == 0 and ic < 2)
                        wload = kb.snap() if first else [("pp", wld_war[ring])]
                        c_g = kb.dma("gpsimd", ("gwg0", "gwg1")[ring],
                                     lambda e, o=wg_sb[ring],
                                     i=weg_in[(ex * 2 * NICM + ic) * 128:(ex * 2 * NICM + ic + 1) * 128, :]:
                                     e.dma_start(out=o[:, :], in_=i),
                                     waits=wload)
                        c_u = kb.dma("gpsimd", ("gwu0", "gwu1")[ring],
                                     lambda e, o=wu_sb[ring],
                                     i=weg_in[(ex * 2 * NICM + NICM + ic) * 128:(ex * 2 * NICM + NICM + ic + 1) * 128, :]:
                                     e.dma_start(out=o[:, :], in_=i),
                                     waits=wload)
                        for tq in range(NTQ):
                            tqb = hbase + tq * SUPF
                            par = tq % 2
                            for c in range(NCH):
                                kb.t(lambda e, o=ps_g[par],
                                     l=wg_sb[ring][:, c * 128:(c + 1) * 128],
                                     r=xfm2[:, c * n_own + tqb:c * n_own + tqb + SUPF],
                                     st=(c == 0), sp=(c == NCH - 1):
                                     e.matmul(o[:, :], l, r, start=st, stop=sp),
                                     waits=[(("gwg0", "gwg1")[ring], c_g), ("vv", gu_war[par]), ("pp", he_war)],
                                     mark=(c == NCH - 1))
                            m_g = kb.cnt["pp"]
                            for c in range(NCH):
                                kb.t(lambda e, o=ps_u[par],
                                     l=wu_sb[ring][:, c * 128:(c + 1) * 128],
                                     r=xfm2[:, c * n_own + tqb:c * n_own + tqb + SUPF],
                                     st=(c == 0), sp=(c == NCH - 1):
                                     e.matmul(o[:, :], l, r, start=st, stop=sp),
                                     waits=[(("gwu0", "gwu1")[ring], c_u), ("vv", gu_war[par]), ("pp", he_war)],
                                     mark=(c == NCH - 1))
                            m_u = kb.cnt["pp"]
                            if tq == NTQ - 1:
                                wld_war[ring] = m_u
                            m_si = kb.s(lambda e, o=silu_m[par], i=ps_g[par]:
                                        e.activation(o[:, :], i[:, :], AF.Silu),
                                        waits=[("pp", m_g), ("vv", silu_war[par])])
                            kb.v(lambda e, o=he[:, ic * HF + tq * SUPF:ic * HF + (tq + 1) * SUPF],
                                 i=ps_u[par], j=silu_m[par]:
                                 e.tensor_tensor(o, i[:, :], j[:, :], ALU.mult),
                                 waits=[("pp", m_u), ("ss", m_si)])
                            gu_war[par] = kb.cnt["vv"]
                            silu_war[par] = gu_war[par]
                    m_he = kb.cnt["vv"]
                    for jb in range(NJBM):
                        ring = jb % 2
                        first = (hf == 0 and ex == 0 and jb < 2)
                        wload = kb.snap() if first else [("pp", wd_war[ring])]
                        c_d = kb.dma("gpsimd", ("gwd0", "gwd1")[ring],
                                     lambda e, o=wedb[ring],
                                     i=wed_in[(ex * NJBM + jb) * 128:(ex * NJBM + jb + 1) * 128, :]:
                                     e.dma_start(out=o[:, :], in_=i),
                                     waits=wload)
                        for tc in range(NTCH):
                            par = tc % 2
                            tcg = (hbase + tc * 128) // 128
                            for ic in range(NICM):
                                kb.t(lambda e, o=ps_m[par],
                                     l=he[:, ic * HF + tc * 128:ic * HF + (tc + 1) * 128],
                                     r=wedb[ring][:, ic * JBM:(ic + 1) * JBM],
                                     st=(ic == 0), sp=(ic == NICM - 1):
                                     e.matmul(o[:, :], l, r, start=st, stop=sp),
                                     waits=[(("gwd0", "gwd1")[ring], c_d), ("vv", m_he), ("vv", d_war[par])],
                                     mark=(ic == NICM - 1))
                            m_mm = kb.cnt["pp"]
                            if tc == NTCH - 1:
                                wd_war[ring] = m_mm
                            kb.v(lambda e, o=acc[:, tc * D + jb * JBM:tc * D + (jb + 1) * JBM],
                                 i=ps_m[par], s1=w_all[:, tcg * 8 + ex:tcg * 8 + ex + 1],
                                 a2=acc[:, tc * D + jb * JBM:tc * D + (jb + 1) * JBM]:
                                 e.scalar_tensor_tensor(o, i[:, :], s1, a2, ALU.mult, ALU.add),
                                 waits=[("pp", m_mm), ("sr2", c_acc)])
                            d_war[par] = kb.cnt["vv"]
                    he_war = max(wd_war)
                m_done = kb.cnt["vv"]
                for tc in range(NTCH):
                    trow = hbase + tc * 128
                    kb.dma("sync", "sout",
                           lambda e, o=out_t[trow:trow + 128, :], i=acc[:, tc * D:(tc + 1) * D]:
                           e.dma_start(out=o, in_=i),
                           waits=[("vv", m_done)])
            kb._op("sync", lambda e: e.nop(), [("sout", kb.cnt["sout"])], None, 1)
            kb.flush(barrier=False)

    return nc


def _prep_core_inputs(core, x_np, Wq, Wk, Wv, Wo, Wr, Wsg, Wsgu, Wsd, Weg, Wed, T):
    b = core // 4
    hq = core % 4
    n_own = T // 4
    own0 = hq * n_own
    cs = hq * 512
    f32 = np.float32
    strip = np.zeros((128, 896), f32)
    for kk in range(128):
        strip[kk, :kk + 384] = NEGBIG
    oh = np.zeros((128, 4), f32)
    oh[:, hq] = 1.0

    def img(a, nch=16):
        n = a.shape[1]
        return np.ascontiguousarray(
            a.reshape(nch, 128, n).transpose(1, 0, 2).reshape(128, nch * n))

    wsgu_t = Wsgu.reshape(16, 128, 2 * NICS_C, 128).transpose(2, 1, 0, 3) \
        .reshape(2 * NICS_C * 128, 16 * 128)
    wsd_t = Wsd.reshape(NICS_C, 128, NJBS_C, JBS_C).transpose(2, 1, 0, 3) \
        .reshape(NJBS_C * 128, NICS_C * JBS_C)
    weg_t = Weg.reshape(NEXP, 16, 128, 2 * 11, 128).transpose(0, 3, 2, 1, 4) \
        .reshape(NEXP * 2 * 11 * 128, 16 * 128)
    wed_t = Wed.reshape(NEXP, 11, 128, NJBM_C, JBM_C).transpose(0, 3, 2, 1, 4) \
        .reshape(NEXP * NJBM_C * 128, 11 * JBM_C)
    d = {
        "x": np.ascontiguousarray(x_np[b, :T], f32),
        "x_own": np.ascontiguousarray(x_np[b, own0:own0 + n_own], f32),
        "wq": img(np.ascontiguousarray(Wq[cs:cs + 512, :].T)),
        "wk": img(np.ascontiguousarray(Wk[cs:cs + 512, :].T)),
        "wv": img(np.ascontiguousarray(Wv[cs:cs + 512, :].T)),
        "wos": img(np.ascontiguousarray(Wo.T)),
        "wrsg": img(np.ascontiguousarray(np.concatenate([Wr.T, Wsg.T], axis=1))),
        "wsgu": np.ascontiguousarray(wsgu_t),
        "wsd": np.ascontiguousarray(wsd_t),
        "weg": np.ascontiguousarray(weg_t),
        "wed": np.ascontiguousarray(wed_t),
        "ident": np.eye(128, dtype=f32),
        "ones_mat": np.ones((128, 128), f32),
        "strip": strip,
        "oh": oh,
    }
    return d


def kernel(L0_attn_x, L0_attn_Wq, L0_attn_Wk, L0_attn_Wv, L0_attn_Wo,
           L0_ffn_W_router, L0_ffn_W_sgate, L0_ffn_Ws_gate_up, L0_ffn_Ws_down,
           L0_ffn_We_gate_up, L0_ffn_We_down):
    T = 8192
    n_own = T // 4
    x = np.asarray(L0_attn_x, np.float32)
    args = (np.asarray(L0_attn_Wq, np.float32), np.asarray(L0_attn_Wk, np.float32),
            np.asarray(L0_attn_Wv, np.float32), np.asarray(L0_attn_Wo, np.float32),
            np.asarray(L0_ffn_W_router, np.float32), np.asarray(L0_ffn_W_sgate, np.float32),
            np.asarray(L0_ffn_Ws_gate_up, np.float32), np.asarray(L0_ffn_Ws_down, np.float32),
            np.asarray(L0_ffn_We_gate_up, np.float32), np.asarray(L0_ffn_We_down, np.float32))
    in_maps = [_prep_core_inputs(c, x, *args, T) for c in range(8)]
    nc = build_nc(T=T)
    res = run_bass_kernel_spmd(nc, in_maps, core_ids=list(range(8)))
    out = np.empty((2, T, D), np.float32)
    for c in range(8):
        b = c // 4
        own0 = (c % 4) * n_own
        out[b, own0:own0 + n_own] = res.results[c]["out"]
    return out


# version 08 (final)
# speedup vs baseline: 1.0684x; 1.0029x over previous
import numpy as np
from contextlib import ExitStack

import concourse.bass as bass
import concourse.mybir as mybir
from concourse.bass_utils import run_bass_kernel_spmd

F32 = mybir.dt.float32
BF16 = mybir.dt.bfloat16
AF = mybir.ActivationFunctionType
ALU = mybir.AluOpType
AX = mybir.AxisListType

ENGS = ("sync", "scalar", "vector", "tensor", "gpsimd")

D = 2048
HD = 128
NCH = D // 128
NHL = 4
ISH = 5632
NICS = ISH // 128
IMOE = 1408
NICM = IMOE // 128
NEXP = 8
NICS_C = 44
NICM_C = 11
JBS_C = 256
NJBS_C = 2048 // JBS_C
JBM_C = 512
NJBM_C = 2048 // JBM_C
SCALE = float(np.float32(1.0) / np.float32(np.sqrt(np.float32(128.0))))
NEGBIG = -30000.0
EPS = 1e-5

SEMS = ["vv", "ss", "pp", "cc", "sc0",
        "sx", "sxg", "gw", "gww", "swk0", "swk1", "swq0", "swq1",
        "skh", "sqh", "szs",
        "srs", "srsb", "sxo", "sxob", "sx1", "sx1b",
        "sr10", "sr11", "sx20", "sx21", "sr2", "sout",
        "gwg0", "gwg1", "gwu0", "gwu1", "gwd0", "gwd1"]


def _s3(sb, c):
    return sb.ap().rearrange("p (c n) -> p c n", c=c)


class K:

    def __init__(self, nc):
        self.nc = nc
        self.cnt = {s: 0 for s in SEMS}
        self.lastw = {e: {} for e in ENGS}
        self.sems = {}
        self.ops = {e: [] for e in ENGS}
        self.pending_barrier = None
        self.barrier_done = set()

    def _op(self, eng, fn, waits, inc, k):
        wl = []
        if self.pending_barrier is not None and eng not in self.barrier_done:
            waits = self.pending_barrier + list(waits)
            self.barrier_done.add(eng)
        lw = self.lastw[eng]
        for s, c in waits:
            if c > 0 and lw.get(s, 0) < c:
                wl.append((s, c))
                lw[s] = c
        self.ops[eng].append((fn, wl, inc, k))
        if inc is None:
            return None
        self.cnt[inc] += k
        return self.cnt[inc]

    def v(self, fn, waits=()):
        return self._op("vector", fn, list(waits) + [("vv", self.cnt["vv"])], "vv", 1)

    def s(self, fn, waits=()):
        return self._op("scalar", fn, list(waits) + [("ss", self.cnt["ss"])], "ss", 1)

    def t(self, fn, waits=(), mark=False):
        return self._op("tensor", fn, waits, "pp" if mark else None, 1)

    def dma(self, eng, sem, fn, waits=()):
        return self._op(eng, fn, waits, sem, 16)

    def coll(self, fn, waits=()):
        return self._op("gpsimd", fn, waits, "cc", 1)

    def snap(self, extra=()):
        out = [(s, self.cnt[s]) for s in ("vv", "ss", "pp")]
        for s in extra:
            out.append((s, self.cnt[s]))
        return out

    def flush(self, barrier=True):
        nc = self.nc
        ops = self.ops
        sems = self.sems
        with nc.Block() as blk:
            for e in ENGS:
                lst = ops[e]
                if not lst:
                    continue

                def runner(eng, lst=lst):
                    for fn, wl, inc, k in lst:
                        for s, c in wl:
                            eng.wait_ge(sems[s], c)
                        inst = fn(eng)
                        if inc is not None:
                            inst.then_inc(sems[inc], k)

                getattr(blk, e)(runner)
        self.ops = {e: [] for e in ENGS}
        if barrier:
            self.pending_barrier = [(s, self.cnt[s]) for s in SEMS if self.cnt[s] > 0]
            self.barrier_done = set()


def emit_ln(kb, xt, tmp, out_ap, x_waits, d_inv, norm_waits=()):
    m_sum = kb.v(lambda e, a=tmp["s1"], b=xt: e.reduce_sum(a[:, :], b, axis=AX.X),
                 waits=x_waits)
    m_sq = kb.s(lambda e, o=tmp["sq"], i=xt, a=tmp["s2"]:
                e.activation(o[:, :], i, AF.Square, accum_out=a[:, :]),
                waits=x_waits)
    kb.v(lambda e, o=tmp["mean"], i=tmp["s1"]: e.tensor_scalar_mul(o[:, :], i[:, :], d_inv))
    kb.v(lambda e, o=tmp["m2"], i=tmp["mean"]: e.tensor_mul(o[:, :], i[:, :], i[:, :]))
    kb.v(lambda e, o=tmp["var"], i=tmp["s2"], j=tmp["m2"]:
         e.scalar_tensor_tensor(o[:, :], i[:, :], d_inv, j[:, :], ALU.mult, ALU.subtract),
         waits=[("ss", m_sq)])
    m_var = kb.cnt["vv"]
    m_sqrt = kb.s(lambda e, o=tmp["rstd0"], i=tmp["var"], b=kb.eps:
                  e.activation(o[:, :], i[:, :], AF.Sqrt, bias=b[:, 0:1]),
                  waits=[("vv", m_var)])
    kb.v(lambda e, o=tmp["rstd"], i=tmp["rstd0"]: e.reciprocal(o[:, :], i[:, :]),
         waits=[("ss", m_sqrt)])
    m_norm = kb.v(lambda e, o=out_ap, i=xt, m=tmp["mean"], r=tmp["rstd"]:
                  e.tensor_scalar(o, i, m[:, 0:1], r[:, 0:1], ALU.subtract, ALU.mult),
                  waits=norm_waits)
    return m_norm


def build_nc(T=8192, detect_race=True):
    NSUP = T // 512
    NQB = T // 512
    n_own = T // 4
    NTS = max(1, n_own // 512)
    HF = n_own // 2
    SUPF = min(512, HF)
    NTQ = HF // SUPF
    NTCH = HF // 128
    JBS = JBS_C
    NJBS = NJBS_C
    JBM = JBM_C
    NJBM = NJBM_C
    NOC = n_own // 128
    d_inv = 1.0 / D

    nc = bass.Bass("TRN2", target_bir_lowering=False, debug=False,
                   detect_race_conditions=detect_race)

    x_in = nc.dram_tensor("x", [T, D], F32, kind="ExternalInput").ap()
    xo_in = nc.dram_tensor("x_own", [n_own, D], F32, kind="ExternalInput").ap()
    wq_in = nc.dram_tensor("wq", [128, NCH * 512], F32, kind="ExternalInput").ap()
    wk_in = nc.dram_tensor("wk", [128, NCH * 512], F32, kind="ExternalInput").ap()
    wv_in = nc.dram_tensor("wv", [128, NCH * 512], F32, kind="ExternalInput").ap()
    wos_in = nc.dram_tensor("wos", [128, NCH * D], F32, kind="ExternalInput").ap()
    wrsg_in = nc.dram_tensor("wrsg", [128, NCH * 9], F32, kind="ExternalInput").ap()
    wsgu_in = nc.dram_tensor("wsgu", [2 * NICS * 128, NCH * 128], F32, kind="ExternalInput").ap()
    wsd_in = nc.dram_tensor("wsd", [NJBS_C * 128, NICS * JBS_C], F32, kind="ExternalInput").ap()
    weg_in = nc.dram_tensor("weg", [NEXP * 2 * NICM * 128, NCH * 128], F32, kind="ExternalInput").ap()
    wed_in = nc.dram_tensor("wed", [NEXP * NJBM_C * 128, NICM * JBM_C], F32, kind="ExternalInput").ap()
    ident_in = nc.dram_tensor("ident", [128, 128], F32, kind="ExternalInput").ap()
    onesm_in = nc.dram_tensor("ones_mat", [128, 128], F32, kind="ExternalInput").ap()
    strip_in = nc.dram_tensor("strip", [128, 896], F32, kind="ExternalInput").ap()
    oh_in = nc.dram_tensor("oh", [128, 4], F32, kind="ExternalInput").ap()
    out_t = nc.dram_tensor("out", [n_own, D], F32, kind="ExternalOutput").ap()

    k_scr = nc.dram_tensor("k_scr", [512, T], BF16).ap()
    q_scr = nc.dram_tensor("q_scr", [512, T], BF16).ap()
    z_rs = nc.dram_tensor("z_rs", [NHL * 2 * 2048, 1024], BF16).ap()
    zg_scr = nc.dram_tensor("zg_scr", [NHL * 2 * 512, 1024], BF16).ap()
    x1_scr = nc.dram_tensor("x1_scr", [n_own, D], F32).ap()
    x2_scr = nc.dram_tensor("x2_scr", [n_own, D], F32).ap()

    kb = K(nc)

    with ExitStack() as outer:
        E = outer.enter_context
        for s in SEMS:
            kb.sems[s] = E(nc.semaphore(s))

        ident = E(nc.sbuf_tensor("ident_bf", [128, 128], BF16))
        wrsg_f = E(nc.sbuf_tensor("wrsg_f", [128, NCH * 9], F32))
        wrhi = E(nc.sbuf_tensor("wrhi", [128, NCH * 9], BF16))
        wrhi32 = E(nc.sbuf_tensor("wrhi32", [128, NCH * 9], F32))
        wrlo = E(nc.sbuf_tensor("wrlo", [128, NCH * 9], BF16))
        w_all = E(nc.sbuf_tensor("w_all", [128, NOC * 8], F32))
        eps_sb = E(nc.sbuf_tensor("eps_sb", [128, 1], F32))
        kb.eps = eps_sb
        sg_all = E(nc.sbuf_tensor("sg_all", [128, NOC], F32))
        oh_sb = E(nc.sbuf_tensor("oh_sb", [128, 4], F32))

        c_ident = kb.dma("gpsimd", "gw", lambda e: e.dma_start(out=ident[:, :], in_=ident_in))
        kb.dma("sync", "sc0", lambda e: e.dma_start(out=oh_sb[:, :], in_=oh_in))
        c_wrsg = kb.dma("sync", "sc0", lambda e: e.dma_start(
            out=wrsg_f[:, :], in_=wrsg_in))
        kb.v(lambda e, o=eps_sb: e.memset(o[:, :], EPS))
        kb.v(lambda e: e.tensor_copy(wrhi[:, :], wrsg_f[:, :]), waits=[("sc0", c_wrsg)])
        kb.v(lambda e: e.tensor_copy(wrhi32[:, :], wrhi[:, :]))
        kb.v(lambda e: e.tensor_sub(wrlo[:, :], wrsg_f[:, :], wrhi32[:, :]))

        v_stack = ExitStack()
        v_sb = v_stack.enter_context(nc.sbuf_tensor("v_sb", [128, NHL * T], BF16))
        wos_stack = ExitStack()

        with ExitStack() as sA:
            EA = sA.enter_context
            wqs = EA(nc.sbuf_tensor("wqs", [128, NCH * 512], BF16))
            wks = EA(nc.sbuf_tensor("wks", [128, NCH * 512], BF16))
            wvs = EA(nc.sbuf_tensor("wvs", [128, NCH * 512], BF16))
            x_t = [EA(nc.sbuf_tensor(f"x_t{i}", [128, D], F32)) for i in range(2)]
            xln_t = [EA(nc.sbuf_tensor(f"xln_t{i}", [128, D], BF16)) for i in range(2)]
            xfm = EA(nc.sbuf_tensor("xfm", [128, NCH * 512], BF16))
            sq_sb = EA(nc.sbuf_tensor("sq_sb", [128, D], F32))
            kst = [EA(nc.sbuf_tensor(f"kst{i}", [128, 512], BF16)) for i in range(4)]
            qst = [EA(nc.sbuf_tensor(f"qst{i}", [128, 512], BF16)) for i in range(4)]
            tmpA = [{n: EA(nc.sbuf_tensor(f"lnA{p}_{n}", [128, 1], F32))
                     for n in ("s1", "s2", "mean", "m2", "var", "rstd0", "rstd")}
                    for p in range(2)]
            for p in range(2):
                tmpA[p]["sq"] = sq_sb
            ps_t = [EA(nc.psum_tensor(f"ps_t{i}", [128, 128], BF16)) for i in range(4)]
            ps_kq = [EA(nc.psum_tensor(f"ps_kq{i}", [128, 512], F32)) for i in range(2)]
            ps_v = [EA(nc.psum_tensor(f"ps_v{i}", [128, 512], F32)) for i in range(2)]

            c_w3 = 0
            for w_sb, w_src in ((wqs, wq_in), (wks, wk_in), (wvs, wv_in)):
                c_w3 = kb.dma("gpsimd", "gww",
                              lambda e, o=w_sb, i=w_src: e.dma_start(out=o[:, :], in_=i))

            ln_marks = {-2: (0, 0), -1: (0, 0)}
            tp_done_t = {-2: 0, -1: 0}
            tposev_mark = [0, 0, 0, 0]
            kq_war = [0, 0]
            v_war = [0, 0]
            kst_war = [0, 0, 0, 0]
            qst_war = [0, 0, 0, 0]
            tr_cnt = 0
            XQ = [("sync", "sx"), ("scalar", "sxg")]

            for st_i in range(NSUP):
                t0 = st_i * 512
                for a in range(4):
                    t = st_i * 4 + a
                    par = t % 2
                    vvf, ssf = ln_marks[t - 2]
                    qn, sn = XQ[par]
                    c_x = kb.dma(qn, sn,
                                 lambda e, o=x_t[par], i=x_in[t * 128:(t + 1) * 128, :]:
                                 e.dma_start(out=o[:, :], in_=i),
                                 waits=[("vv", vvf), ("ss", ssf)])
                    m_n = emit_ln(kb, x_t[par][:, :], tmpA[par], xln_t[par][:, :],
                                  [(sn, c_x)], d_inv,
                                  norm_waits=[("pp", tp_done_t[t - 2])])
                    ln_marks[t] = (kb.cnt["vv"], kb.cnt["ss"])
                    for c in range(NCH):
                        ptp = tr_cnt % 4
                        tr_cnt += 1
                        m_t = kb.t(lambda e, o=ps_t[ptp],
                                   i=xln_t[par][:, c * 128:(c + 1) * 128], idn=ident:
                                   e.transpose(o[:, :], i, idn[:, :]),
                                   waits=[("vv", m_n), ("ss", tposev_mark[ptp]),
                                          ("gw", c_ident)],
                                   mark=True)
                        kb.s(lambda e, o=xfm[:, c * 512 + a * 128: c * 512 + (a + 1) * 128],
                             i=ps_t[ptp]: e.copy(o, i[:, :]),
                             waits=[("pp", m_t)])
                        tposev_mark[ptp] = kb.cnt["ss"]
                    tp_done_t[t] = kb.cnt["pp"]
                m_evac_all = kb.cnt["ss"]

                for w_sb, stg, scr, sem1, war, stq in (
                        (wks, kst, k_scr, "swk0", kst_war, "sync"),
                        (wqs, qst, q_scr, "swq0", qst_war, "scalar")):
                    for j in range(4):
                        par = j % 2
                        for c in range(NCH):
                            kb.t(lambda e, o=ps_kq[par],
                                 l=w_sb[:, c * 512 + j * 128: c * 512 + (j + 1) * 128],
                                 r=xfm[:, c * 512:(c + 1) * 512],
                                 st=(c == 0), sp=(c == NCH - 1):
                                 e.matmul(o[:, :], l, r, start=st, stop=sp),
                                 waits=[("ss", m_evac_all), ("gww", c_w3),
                                        ("vv", kq_war[par])],
                                 mark=(c == NCH - 1))
                        m_mm = kb.cnt["pp"]
                        kb.v(lambda e, o=stg[j], i=ps_kq[par]: e.tensor_copy(o[:, :], i[:, :]),
                             waits=[("pp", m_mm), (sem1, war[j])])
                        m_ev = kb.cnt["vv"]
                        kq_war[par] = m_ev
                        war[j] = kb.dma(stq, sem1,
                                        lambda e, o=scr[j * 128:(j + 1) * 128, t0:t0 + 512],
                                        i=stg[j]: e.dma_start(out=o, in_=i[:, :]),
                                        waits=[("vv", m_ev)])
                for a in range(4):
                    par = a % 2
                    for c in range(NCH):
                        kb.t(lambda e, o=ps_v[par],
                             l=xfm[:, c * 512 + a * 128: c * 512 + (a + 1) * 128],
                             r=wvs[:, c * 512:(c + 1) * 512],
                             st=(c == 0), sp=(c == NCH - 1):
                             e.matmul(o[:, :], l, r, start=st, stop=sp),
                             waits=[("ss", m_evac_all), ("gww", c_w3), ("vv", v_war[par])],
                             mark=(c == NCH - 1))
                    m_mm = kb.cnt["pp"]
                    kc = t0 // 128 + a
                    for h in range(NHL):
                        kb.v(lambda e, o=v_sb[:, h * T + kc * 128:h * T + (kc + 1) * 128],
                             i=ps_v[par][:, h * 128:(h + 1) * 128]: e.tensor_copy(o, i),
                             waits=[("pp", m_mm)])
                    v_war[par] = kb.cnt["vv"]
            kb.flush()

        EW = wos_stack.enter_context
        wos_sb = EW(nc.sbuf_tensor("wos_sb", [128, NCH * D], BF16, side="right"))

        with ExitStack() as sB:
            EB = sB.enter_context
            NR = 4
            kh = EB(nc.sbuf_tensor("kh", [128, T], BF16))
            qh = EB(nc.sbuf_tensor("qh", [128, T], BF16))
            onesm = EB(nc.sbuf_tensor("onesm_bf", [128, 128], BF16))
            strip = EB(nc.sbuf_tensor("strip_f", [128, 896], F32))
            p_ring = [EB(nc.sbuf_tensor(f"p_ring{i}", [128, 1024], BF16)) for i in range(NR)]
            rbc_sb = [EB(nc.sbuf_tensor(f"rbc_sb{i}", [128, 512], F32)) for i in range(2)]
            zring = [EB(nc.sbuf_tensor(f"zring{i}", [128, 4 * 512], BF16)) for i in range(2)]
            ps_s = [EB(nc.psum_tensor(f"ps_s{i}", [128, 1024], F32)) for i in range(2)]
            ps_o = [EB(nc.psum_tensor(f"ps_o{i}", [128, 512], F32)) for i in range(2)]
            ps_lb = [EB(nc.psum_tensor(f"ps_lb{i}", [128, 512], F32)) for i in range(2)]

            c_onesm = kb.dma("gpsimd", "gw", lambda e: e.dma_start(
                out=onesm[:, :], in_=onesm_in))
            c_strip = kb.dma("sync", "sc0", lambda e: e.dma_start(
                out=strip[:, :], in_=strip_in))
            c_wos = kb.dma("gpsimd", "gw", lambda e: e.dma_start(
                out=wos_sb[:, :], in_=wos_in))

            exp_mark = {0: 0, 1: 0}
            ring_free = {}
            o_war = [0, 0]
            lb_war = [0, 0]
            zst_mark = [0, 0]
            pg = 0
            c_kh = c_qh = 0

            def emit_avl(pend, opar, nk, h):
                slot, kc0, m_exp = pend
                for half in (0, 1):
                    kc = kc0 + half
                    kb.t(lambda e, o=ps_o[opar],
                         l=v_sb[:, h * T + kc * 128:h * T + (kc + 1) * 128],
                         r=p_ring[slot][:, half * 512:(half + 1) * 512],
                         st=(kc == 0), sp=(kc == nk - 1):
                         e.matmul(o[:, :], l, r, start=st, stop=sp),
                         waits=[("ss", m_exp)] + ([("vv", o_war[opar])] if kc == 0 else []),
                         mark=(kc == nk - 1))
                    kb.t(lambda e, o=ps_lb[opar], l=onesm,
                         r=p_ring[slot][:, half * 512:(half + 1) * 512],
                         st=(kc == 0), sp=(kc == nk - 1):
                         e.matmul(o[:, :], l[:, :], r, start=st, stop=sp),
                         waits=[("ss", m_exp), ("gw", c_onesm)] +
                               ([("vv", lb_war[opar])] if kc == 0 else []),
                         mark=(half == 1))
                ring_free[slot] = kb.cnt["pp"]

            for h in range(NHL):
                c_kh = kb.dma("sync", "skh",
                              lambda e, o=kh, i=k_scr[h * 128:(h + 1) * 128, :]:
                              e.dma_start(out=o[:, :], in_=i),
                              waits=kb.snap(("swk0",)))
                c_qh = kb.dma("scalar", "sqh",
                              lambda e, o=qh, i=q_scr[h * 128:(h + 1) * 128, :]:
                              e.dma_start(out=o[:, :], in_=i),
                              waits=kb.snap(("swq0",)))
                for qb in range(NQB):
                    nk = 4 * (qb + 1)
                    opar = qb % 2
                    pend = None
                    for j in range(nk // 2):
                        kc0 = 2 * j
                        spar = pg % 2
                        slot = pg % NR
                        pg += 1
                        m_s = 0
                        for half in (0, 1):
                            kc = kc0 + half
                            m_s = kb.t(lambda e, o=ps_s[spar][:, half * 512:(half + 1) * 512],
                                       l=kh[:, kc * 128:(kc + 1) * 128],
                                       r=qh[:, qb * 512:(qb + 1) * 512]:
                                       e.matmul(o, l, r, start=True, stop=True),
                                       waits=[("skh", c_kh), ("sqh", c_qh),
                                              ("ss", exp_mark[spar])],
                                       mark=True)
                        mask_w = []
                        for half in (0, 1):
                            kc = kc0 + half
                            if kc >= 4 * qb:
                                s_off = 384 - 128 * (kc - 4 * qb)
                                kb.v(lambda e, o=ps_s[spar][:, half * 512:(half + 1) * 512],
                                     m=strip[:, s_off:s_off + 512]:
                                     e.tensor_tensor(o, o, m, ALU.add),
                                     waits=[("pp", m_s - 1 + half), ("sc0", c_strip)])
                                mask_w = [("vv", kb.cnt["vv"])]
                        if pend is not None:
                            emit_avl(pend, opar, nk, h)
                        m_exp = kb.s(lambda e, o=p_ring[slot], i=ps_s[spar]:
                                     e.activation(o[:, :], i[:, :], AF.Exp, scale=SCALE),
                                     waits=[("pp", m_s), ("pp", ring_free.get(slot, 0))] + mask_w)
                        exp_mark[spar] = m_exp
                        pend = (slot, kc0, m_exp)
                    emit_avl(pend, opar, nk, h)
                    m_lend = kb.cnt["pp"]
                    kb.v(lambda e, o=rbc_sb[opar], i=ps_lb[opar]:
                         e.reciprocal(o[:, :], i[:, :]),
                         waits=[("pp", m_lend)])
                    lb_war[opar] = kb.cnt["vv"]
                    rp = (h * NQB + qb) % 2
                    for slot4 in range(4):
                        kb.v(lambda e, o=zring[rp][:, slot4 * 512:(slot4 + 1) * 512],
                             i=ps_o[opar], s1=oh_sb[:, slot4:slot4 + 1], j=rbc_sb[opar]:
                             e.scalar_tensor_tensor(o, i[:, :], s1, j[:, :],
                                                    ALU.mult, ALU.mult),
                             waits=([("szs", zst_mark[rp])] if slot4 == 0 else []))
                    o_war[opar] = kb.cnt["vv"]
                    m_z4 = kb.cnt["vv"]
                    q4 = qb // 4
                    ch_q = (qb % 4) // 2
                    cwh = ((qb % 4) % 2) * 512
                    base = (h * 2 + ch_q) * 2048 + q4 * 512
                    zst_mark[rp] = kb.dma(
                        "sync", "szs",
                        lambda e,
                        o=z_rs[base:base + 512, cwh:cwh + 512].rearrange(
                            "(s p) n -> p s n", p=128),
                        i=zring[rp].ap().rearrange("p (s n) -> p s n", s=4):
                        e.dma_start(out=o, in_=i),
                        waits=[("vv", m_z4)])
                    if qb == 13 or qb == 15:
                        ch = 0 if qb == 13 else 1
                        kb.coll(lambda e, h=h, ch=ch: e.collective_compute(
                            "ReduceScatter", ALU.add,
                            replica_groups=[[0, 1, 2, 3], [4, 5, 6, 7]],
                            ins=[z_rs[(h * 2 + ch) * 2048:(h * 2 + ch + 1) * 2048, :]],
                            outs=[zg_scr[(h * 2 + ch) * 512:(h * 2 + ch + 1) * 512, :]]),
                            waits=[("szs", kb.cnt["szs"])])
            kb.flush()
        v_stack.close()

        xfm2 = E(nc.sbuf_tensor("xfm2", [128, NCH * n_own], BF16))

        up_stacks = {0: ExitStack(), 1: ExitStack()}

        def alloc_up(hf):
            EU = up_stacks[hf].enter_context
            return dict(
                wg=[EU(nc.sbuf_tensor(f"wg{i}_{hf}", [128, NCH * 128], BF16, side="right"))
                    for i in range(2)],
                wu=[EU(nc.sbuf_tensor(f"wu{i}_{hf}", [128, NCH * 128], BF16, side="right"))
                    for i in range(2)],
                silu=[EU(nc.sbuf_tensor(f"silu{i}_{hf}", [128, SUPF], F32, side="right"))
                      for i in range(2)],
                ps_g=[EU(nc.psum_tensor(f"ps_g{i}_{hf}", [128, SUPF], F32)) for i in range(2)],
                ps_u=[EU(nc.psum_tensor(f"ps_u{i}_{hf}", [128, SUPF], F32)) for i in range(2)])

        def emit_wload(bufs, ic, waits):
            ring = ic % 2
            c_g = kb.dma("gpsimd", ("gwg0", "gwg1")[ring],
                         lambda e, o=bufs["wg"][ring],
                         i=wsgu_in[ic * 128:(ic + 1) * 128, :]:
                         e.dma_start(out=o[:, :], in_=i),
                         waits=waits)
            c_u = kb.dma("gpsimd", ("gwu0", "gwu1")[ring],
                         lambda e, o=bufs["wu"][ring],
                         i=wsgu_in[(NICS + ic) * 128:(NICS + ic + 1) * 128, :]:
                         e.dma_start(out=o[:, :], in_=i),
                         waits=waits)
            return (c_g, c_u)

        def init_waits():
            return [(s, kb.cnt[s]) for s in
                    ("vv", "ss", "pp", "srs", "srsb", "sxo", "sxob",
                     "sx1", "sx1b", "sr10", "sr11", "sx20", "sx21")]

        st_ss = {}
        x1_cnt = {}

        with ExitStack() as sC:
            EC = sC.enter_context
            zg_sb = [EC(nc.sbuf_tensor(f"zg_sb{i}", [128, NCH * 128], BF16)) for i in range(2)]
            xo_sb = EC(nc.sbuf_tensor("xo_sb", [128, D], F32))
            x1_sb = EC(nc.sbuf_tensor("x1_sb", [128, D], F32))
            xhi = EC(nc.sbuf_tensor("xhi", [128, D], BF16))
            xhi32 = EC(nc.sbuf_tensor("xhi32", [128, D], F32))
            xlo = EC(nc.sbuf_tensor("xlo", [128, D], BF16))
            xlo_fm = EC(nc.sbuf_tensor("xlo_fm", [128, D], BF16))
            pr8 = EC(nc.sbuf_tensor("pr8", [128, 8], F32))
            eq8 = EC(nc.sbuf_tensor("eq8", [128, 8], F32))
            msk8 = EC(nc.sbuf_tensor("msk8", [128, 8], F32))
            mx1 = EC(nc.sbuf_tensor("mx1", [128, 1], F32))
            nmax = EC(nc.sbuf_tensor("nmax", [128, 1], F32))
            se8 = EC(nc.sbuf_tensor("se8", [128, 1], F32))
            rse = EC(nc.sbuf_tensor("rse", [128, 1], F32))
            m2r = EC(nc.sbuf_tensor("m2r", [128, 1], F32))
            tmpc = {n: EC(nc.sbuf_tensor(f"lnc_{n}", [128, 1], F32))
                    for n in ("s1", "s2", "mean", "m2", "var", "rstd0", "rstd")}
            xln32 = xo_sb
            tmpc["sq"] = xhi32
            ps_j = [EC(nc.psum_tensor(f"ps_j{i}", [128, 512], F32)) for i in range(2)]
            ps_t2 = [EC(nc.psum_tensor(f"ps_t2{i}", [128, 128], BF16)) for i in range(2)]
            ps_lg = EC(nc.psum_tensor("ps_lg", [128, 9], F32))

            tposev_mark2 = [0, 0]
            tr2 = 0
            lg_w_vv, lg_w_ss = 0, 0
            tp_mark = 0
            hilo_m = {}
            tev_m = {}
            mx1_m = {}
            zg_read = [0, 0]
            psj_war = [0, 0]
            xo_read = 0
            x1_prev = [("sx1", 0), ("sx1b", 0)]
            CQ = [("sync", "srs", "sxo", "sx1"), ("scalar", "srsb", "sxob", "sx1b")]

            def emit_tr_router(t):
                nonlocal tr2, lg_w_vv, lg_w_ss, tp_mark
                trow = t * 128
                m_hilo = hilo_m[t]
                for c in range(NCH):
                    for src, dst in ((xhi[:, c * 128:(c + 1) * 128],
                                      xfm2[:, c * n_own + trow:c * n_own + trow + 128]),
                                     (xlo[:, c * 128:(c + 1) * 128],
                                      xlo_fm[:, c * 128:(c + 1) * 128])):
                        p2 = tr2 % 2
                        tr2 += 1
                        m_t = kb.t(lambda e, o=ps_t2[p2],
                                   i=src, idn=ident:
                                   e.transpose(o[:, :], i, idn[:, :]),
                                   waits=[("vv", m_hilo), ("ss", tposev_mark2[p2])],
                                   mark=True)
                        kb.s(lambda e, o=dst, i=ps_t2[p2]:
                             e.copy(o, i[:, :]),
                             waits=[("pp", m_t)])
                        tposev_mark2[p2] = kb.cnt["ss"]
                tp_mark = kb.cnt["pp"]
                m_tev = kb.cnt["ss"]
                tev_m[t] = m_tev
                n_mm = 0
                for xsrc, wsrc in ((xfm2, wrhi), (xfm2, wrlo), (xlo_fm, wrhi)):
                    for c in range(NCH):
                        if xsrc is xfm2:
                            l_ap = xfm2[:, c * n_own + trow:c * n_own + trow + 128]
                        else:
                            l_ap = xlo_fm[:, c * 128:(c + 1) * 128]
                        st = (n_mm == 0)
                        sp = (n_mm == 3 * NCH - 1)
                        kb.t(lambda e, o=ps_lg, l=l_ap, r=wsrc[:, c * 9:(c + 1) * 9],
                             st=st, sp=sp: e.matmul(o[:, :], l, r, start=st, stop=sp),
                             waits=[("ss", m_tev), ("vv", lg_w_vv), ("ss", lg_w_ss)],
                             mark=sp)
                        n_mm += 1
                m_lg = kb.cnt["pp"]
                kb.v(lambda e, o=mx1, i=ps_lg: e.reduce_max(o[:, :], i[:, 0:8], axis=AX.X),
                     waits=[("pp", m_lg)])
                kb.v(lambda e, o=nmax, i=mx1: e.tensor_scalar_mul(o[:, :], i[:, :], -1.0))
                m_nm = kb.cnt["vv"]
                m_pe = kb.s(lambda e, o=pr8, i=ps_lg, b=nmax, a=se8:
                            e.activation(o[:, :], i[:, 0:8], AF.Exp,
                                         bias=b[:, 0:1], accum_out=a[:, :]),
                            waits=[("vv", m_nm), ("pp", m_lg)])
                kb.v(lambda e, o=rse, i=se8: e.reciprocal(o[:, :], i[:, :]),
                     waits=[("ss", m_pe)])
                kb.v(lambda e, o=pr8, i=pr8, r=rse:
                     e.tensor_scalar(o[:, :], i[:, :], r[:, 0:1], None, ALU.mult))
                kb.v(lambda e, o=mx1, i=pr8: e.reduce_max(o[:, :], i[:, :], axis=AX.X))
                kb.v(lambda e, o=eq8, i=pr8, m=mx1:
                     e.tensor_scalar(o[:, :], i[:, :], m[:, 0:1], None, ALU.is_equal))
                kb.v(lambda e, o=eq8, i=pr8:
                     e.scalar_tensor_tensor(o[:, :], o[:, :], -1e30, i[:, :], ALU.mult, ALU.add))
                kb.v(lambda e, o=m2r, i=eq8: e.reduce_max(o[:, :], i[:, :], axis=AX.X))
                kb.v(lambda e, o=msk8, i=pr8, m=m2r:
                     e.tensor_scalar(o[:, :], i[:, :], m[:, 0:1], None, ALU.is_ge))
                kb.v(lambda e, o=w_all[:, t * 8:(t + 1) * 8], i=pr8, m=msk8:
                     e.tensor_mul(o, i[:, :], m[:, :]))
                lg_w_vv = kb.cnt["vv"]
                lg_w_ss = kb.s(lambda e, o=sg_all[:, t:t + 1], i=ps_lg:
                               e.activation(o, i[:, 8:9], AF.Sigmoid),
                               waits=[("pp", m_lg)])
                if t % 4 == 3:
                    st_ss[t // 4] = kb.cnt["ss"]

            for t in range(NOC):
                par = t % 2
                trow = t * 128
                ch_t = t // 8
                col = (t * 128) % 1024
                qn, s_zg, s_xo, s_x1 = CQ[par]
                c_zg = 0
                for hh in range(4):
                    c_zg = kb.dma(qn, s_zg,
                                  lambda e,
                                  o=zg_sb[par].ap().rearrange("p (h r n) -> p (h r) n",
                                                              h=4, r=4)[
                                      :, hh * 4:(hh + 1) * 4, :],
                                  i=zg_scr[(hh * 2 + ch_t) * 512:(hh * 2 + ch_t + 1) * 512,
                                           col:col + 128].rearrange(
                                      "(r p) n -> p r n", p=128):
                                  e.dma_start(out=o, in_=i),
                                  waits=[("cc", 7 + ch_t), ("pp", zg_read[par])])
                c_xo = kb.dma(qn, s_xo,
                              lambda e, o=xo_sb, i=xo_in[trow:trow + 128, :]:
                              e.dma_start(out=o[:, :], in_=i),
                              waits=[("vv", xo_read)])
                ev = []
                for jh in range(2):
                    for g in range(16):
                        cb = (g % 4) * 4 + g // 4
                        for j2 in range(2):
                            j = jh * 2 + j2
                            kb.t(lambda e, o=ps_j[j2],
                                 l=zg_sb[par][:, cb * 128:(cb + 1) * 128],
                                 r=wos_sb[:, g * D + j * 512:g * D + (j + 1) * 512],
                                 st=(g == 0), sp=(g == 15):
                                 e.matmul(o[:, :], l, r, start=st, stop=sp),
                                 waits=[(s_zg, c_zg), ("gw", c_wos)] +
                                       ([("vv", psj_war[j2])] if g == 0 else []),
                                 mark=(g == 15))
                    m_j = kb.cnt["pp"]
                    for j2 in range(2):
                        j = jh * 2 + j2
                        kb.v(lambda e, o=x1_sb[:, j * 512:(j + 1) * 512],
                             i=ps_j[j2], j3=xo_sb[:, j * 512:(j + 1) * 512]:
                             e.scalar_tensor_tensor(o, i[:, :], 1.0, j3,
                                                    ALU.mult, ALU.add),
                             waits=[("pp", m_j - 1 + j2), (s_xo, c_xo),
                                    x1_prev[par]])
                        psj_war[j2] = kb.cnt["vv"]
                zg_read[par] = kb.cnt["pp"]
                m_x1 = kb.cnt["vv"]
                c_x1 = kb.dma(qn, s_x1,
                              lambda e, o=x1_scr[trow:trow + 128, :], i=x1_sb:
                              e.dma_start(out=o, in_=i[:, :]),
                              waits=[("vv", m_x1)])
                x1_prev[par] = (s_x1, c_x1)
                if t > 0:
                    emit_tr_router(t - 1)
                emit_ln(kb, x1_sb[:, :], tmpc, xln32[:, :], [("vv", m_x1)], d_inv)
                kb.v(lambda e, o=xhi, i=xln32: e.tensor_copy(o[:, :], i[:, :]),
                     waits=[("pp", tp_mark)])
                kb.v(lambda e, o=xhi32, i=xhi: e.tensor_copy(o[:, :], i[:, :]))
                kb.v(lambda e, o=xlo, i=xln32, j=xhi32: e.tensor_sub(o[:, :], i[:, :], j[:, :]))
                hilo_m[t] = kb.cnt["vv"]
                xo_read = hilo_m[t]
                if t % 4 == 3:
                    x1_cnt[t // 4] = (kb.cnt["sx1"], kb.cnt["sx1b"])
            emit_tr_router(NOC - 1)
            kb.flush(barrier=False)
        wos_stack.close()
        c_end = init_waits()

        with ExitStack() as sE:
            EE = sE.enter_context
            hsh = EE(nc.sbuf_tensor("hsh", [128, NICS * HF], BF16))
            for hf in range(2):
                hbase = hf * HF
                half_pp = kb.cnt["pp"]
                bufs = alloc_up(hf)
                wg_sb, wu_sb, silu_sb = bufs["wg"], bufs["wu"], bufs["silu"]
                ps_g, ps_u = bufs["ps_g"], bufs["ps_u"]
                up1_init = c_end if hf == 0 else init_waits()
                v0u = kb.cnt["vv"]
                p0u = kb.cnt["pp"]
                wld_war = [p0u, p0u]
                gu_war = [v0u, v0u]
                silu_war = [v0u, v0u]
                for ic in range(NICS):
                    ring = ic % 2
                    wload = up1_init if ic < 2 else [("pp", wld_war[ring])]
                    c_g, c_u = emit_wload(bufs, ic, wload)
                    for tq in range(NTQ):
                        tqb = hbase + tq * SUPF
                        par = tq % 2
                        for c in range(NCH):
                            kb.t(lambda e, o=ps_g[par],
                                 l=wg_sb[ring][:, c * 128:(c + 1) * 128],
                                 r=xfm2[:, c * n_own + tqb:c * n_own + tqb + SUPF],
                                 st=(c == 0), sp=(c == NCH - 1):
                                 e.matmul(o[:, :], l, r, start=st, stop=sp),
                                 waits=[(("gwg0", "gwg1")[ring], c_g), ("vv", gu_war[par]),
                                        ("ss", st_ss[hf * 2 + tq])],
                                 mark=(c == NCH - 1))
                        m_g = kb.cnt["pp"]
                        for c in range(NCH):
                            kb.t(lambda e, o=ps_u[par],
                                 l=wu_sb[ring][:, c * 128:(c + 1) * 128],
                                 r=xfm2[:, c * n_own + tqb:c * n_own + tqb + SUPF],
                                 st=(c == 0), sp=(c == NCH - 1):
                                 e.matmul(o[:, :], l, r, start=st, stop=sp),
                                 waits=[(("gwu0", "gwu1")[ring], c_u), ("vv", gu_war[par]),
                                        ("ss", st_ss[hf * 2 + tq])],
                                 mark=(c == NCH - 1))
                        m_u = kb.cnt["pp"]
                        if tq == NTQ - 1:
                            wld_war[ring] = m_u
                        m_si = kb.s(lambda e, o=silu_sb[par], i=ps_g[par]:
                                    e.activation(o[:, :], i[:, :], AF.Silu),
                                    waits=[("pp", m_g), ("vv", silu_war[par])])
                        kb.v(lambda e, o=hsh[:, ic * HF + tq * SUPF:ic * HF + (tq + 1) * SUPF],
                             i=ps_u[par], j=silu_sb[par]:
                             e.tensor_tensor(o, i[:, :], j[:, :], ALU.mult),
                             waits=[("pp", m_u), ("ss", m_si), ("pp", half_pp)])
                        gu_war[par] = kb.cnt["vv"]
                        silu_war[par] = gu_war[par]
                m_hsh = kb.cnt["vv"]
                up_stacks[hf].close()
                with ExitStack() as sDn:
                    ED = sDn.enter_context
                    ICQ = NICS // 4
                    wsdb = ED(nc.sbuf_tensor(f"wsdb_{hf}", [128, NICS * JBS], BF16))
                    x1s = [ED(nc.sbuf_tensor(f"x1s{i}_{hf}", [128, JBS], F32)) for i in range(2)]
                    x2s = [ED(nc.sbuf_tensor(f"x2s{i}_{hf}", [128, JBS], F32)) for i in range(2)]
                    ps_d = [ED(nc.psum_tensor(f"ps_d{i}_{hf}", [128, JBS], F32)) for i in range(2)]
                    dn_init = c_end if hf == 0 else up1_init
                    v0d = kb.cnt["vv"]
                    wd_war = 0
                    d_war = [v0d, v0d]
                    x1_war = [v0d, v0d]
                    x2_war = [kb.cnt["sx20"], kb.cnt["sx21"]]
                    for jb in range(NJBS):
                        wload = dn_init if jb == 0 else [("pp", wd_war)]
                        c_dq = []
                        for q in range(4):
                            c_dq.append(kb.dma(
                                "gpsimd", "gwd0",
                                lambda e, o=wsdb[:, q * ICQ * JBS:(q + 1) * ICQ * JBS],
                                i=wsd_in[jb * 128:(jb + 1) * 128,
                                         q * ICQ * JBS:(q + 1) * ICQ * JBS]:
                                e.dma_start(out=o, in_=i),
                                waits=wload if q == 0 else ()))
                        for tc in range(NTCH):
                            par = tc % 2
                            trow = hbase + tc * 128
                            tcg = trow // 128
                            c_x1 = kb.dma("scalar", ("sr10", "sr11")[par],
                                          lambda e, o=x1s[par],
                                          i=x1_scr[trow:trow + 128, jb * JBS:(jb + 1) * JBS]:
                                          e.dma_start(out=o[:, :], in_=i),
                                          waits=(dn_init if jb == 0 and tc < 2 else []) +
                                                [("vv", x1_war[par]),
                                                 ("sx1", x1_cnt[hf * 2 + 1][0]),
                                                 ("sx1b", x1_cnt[hf * 2 + 1][1])])
                            for ic in range(NICS):
                                kb.t(lambda e, o=ps_d[par],
                                     l=hsh[:, ic * HF + tc * 128:ic * HF + (tc + 1) * 128],
                                     r=wsdb[:, ic * JBS:(ic + 1) * JBS],
                                     st=(ic == 0), sp=(ic == NICS - 1):
                                     e.matmul(o[:, :], l, r, start=st, stop=sp),
                                     waits=[("gwd0", c_dq[min(ic // ICQ, 3)]),
                                            ("vv", m_hsh), ("vv", d_war[par])],
                                     mark=(ic == NICS - 1))
                            m_mm = kb.cnt["pp"]
                            if tc == NTCH - 1:
                                wd_war = m_mm
                            kb.v(lambda e, o=x2s[par], i=ps_d[par],
                                 s1=sg_all[:, tcg:tcg + 1], j=x1s[par]:
                                 e.scalar_tensor_tensor(o[:, :], i[:, :], s1, j[:, :],
                                                        ALU.mult, ALU.add),
                                 waits=[("pp", m_mm), (("sr10", "sr11")[par], c_x1),
                                        (("sx20", "sx21")[par], x2_war[par])])
                            d_war[par] = kb.cnt["vv"]
                            x1_war[par] = d_war[par]
                            x2_war[par] = kb.dma(
                                "sync", ("sx20", "sx21")[par],
                                lambda e, o=x2_scr[trow:trow + 128, jb * JBS:(jb + 1) * JBS],
                                i=x2s[par]: e.dma_start(out=o, in_=i[:, :]),
                                waits=[("vv", d_war[par])])
            kb.flush()

        with ExitStack() as sF:
            EF = sF.enter_context
            acc = EF(nc.sbuf_tensor("acc", [128, NTCH * D], F32))
            he = EF(nc.sbuf_tensor("he", [128, NICM * HF], BF16))
            wg_sb = [EF(nc.sbuf_tensor(f"mwg{i}", [128, NCH * 128], BF16)) for i in range(2)]
            wu_sb = [EF(nc.sbuf_tensor(f"mwu{i}", [128, NCH * 128], BF16)) for i in range(2)]
            silu_m = [EF(nc.sbuf_tensor(f"msilu{i}", [128, SUPF], F32)) for i in range(2)]
            wedb = [EF(nc.sbuf_tensor(f"wedb{i}", [128, NICM * JBM], BF16)) for i in range(2)]
            ps_g = [EF(nc.psum_tensor(f"mps_g{i}", [128, SUPF], F32)) for i in range(2)]
            ps_u = [EF(nc.psum_tensor(f"mps_u{i}", [128, SUPF], F32)) for i in range(2)]
            ps_m = [EF(nc.psum_tensor(f"mps_m{i}", [128, JBM], F32)) for i in range(2)]
            for hf in range(2):
                hbase = hf * HF
                accw = kb.snap(("sout",)) if hf else kb.snap()
                c_acc = 0
                for tc in range(NTCH):
                    c_acc = kb.dma("scalar", "sr2",
                                   lambda e, o=acc[:, tc * D:(tc + 1) * D],
                                   i=x2_scr[hbase + tc * 128:hbase + (tc + 1) * 128, :]:
                                   e.dma_start(out=o, in_=i),
                                   waits=accw)
                g0 = kb.cnt["pp"]
                v0 = kb.cnt["vv"]
                wld_war = [g0, g0]
                gu_war = [v0, v0]
                silu_war = [v0, v0]
                wd_war = [g0, g0]
                d_war = [v0, v0]
                he_war = g0
                for ex in range(NEXP):
                    for ic in range(NICM):
                        ring = ic % 2
                        first = (hf == 0 and ex == 0 and ic < 2)
                        wload = kb.snap() if first else [("pp", wld_war[ring])]
                        c_g = kb.dma("gpsimd", ("gwg0", "gwg1")[ring],
                                     lambda e, o=wg_sb[ring],
                                     i=weg_in[(ex * 2 * NICM + ic) * 128:(ex * 2 * NICM + ic + 1) * 128, :]:
                                     e.dma_start(out=o[:, :], in_=i),
                                     waits=wload)
                        c_u = kb.dma("gpsimd", ("gwu0", "gwu1")[ring],
                                     lambda e, o=wu_sb[ring],
                                     i=weg_in[(ex * 2 * NICM + NICM + ic) * 128:(ex * 2 * NICM + NICM + ic + 1) * 128, :]:
                                     e.dma_start(out=o[:, :], in_=i),
                                     waits=wload)
                        for tq in range(NTQ):
                            tqb = hbase + tq * SUPF
                            par = tq % 2
                            for c in range(NCH):
                                kb.t(lambda e, o=ps_g[par],
                                     l=wg_sb[ring][:, c * 128:(c + 1) * 128],
                                     r=xfm2[:, c * n_own + tqb:c * n_own + tqb + SUPF],
                                     st=(c == 0), sp=(c == NCH - 1):
                                     e.matmul(o[:, :], l, r, start=st, stop=sp),
                                     waits=[(("gwg0", "gwg1")[ring], c_g), ("vv", gu_war[par]), ("pp", he_war)],
                                     mark=(c == NCH - 1))
                            m_g = kb.cnt["pp"]
                            for c in range(NCH):
                                kb.t(lambda e, o=ps_u[par],
                                     l=wu_sb[ring][:, c * 128:(c + 1) * 128],
                                     r=xfm2[:, c * n_own + tqb:c * n_own + tqb + SUPF],
                                     st=(c == 0), sp=(c == NCH - 1):
                                     e.matmul(o[:, :], l, r, start=st, stop=sp),
                                     waits=[(("gwu0", "gwu1")[ring], c_u), ("vv", gu_war[par]), ("pp", he_war)],
                                     mark=(c == NCH - 1))
                            m_u = kb.cnt["pp"]
                            if tq == NTQ - 1:
                                wld_war[ring] = m_u
                            m_si = kb.s(lambda e, o=silu_m[par], i=ps_g[par]:
                                        e.activation(o[:, :], i[:, :], AF.Silu),
                                        waits=[("pp", m_g), ("vv", silu_war[par])])
                            kb.v(lambda e, o=he[:, ic * HF + tq * SUPF:ic * HF + (tq + 1) * SUPF],
                                 i=ps_u[par], j=silu_m[par]:
                                 e.tensor_tensor(o, i[:, :], j[:, :], ALU.mult),
                                 waits=[("pp", m_u), ("ss", m_si)])
                            gu_war[par] = kb.cnt["vv"]
                            silu_war[par] = gu_war[par]
                    m_he = kb.cnt["vv"]
                    for jb in range(NJBM):
                        ring = jb % 2
                        first = (hf == 0 and ex == 0 and jb < 2)
                        wload = kb.snap() if first else [("pp", wd_war[ring])]
                        c_d = kb.dma("gpsimd", ("gwd0", "gwd1")[ring],
                                     lambda e, o=wedb[ring],
                                     i=wed_in[(ex * NJBM + jb) * 128:(ex * NJBM + jb + 1) * 128, :]:
                                     e.dma_start(out=o[:, :], in_=i),
                                     waits=wload)
                        for tc in range(NTCH):
                            par = tc % 2
                            tcg = (hbase + tc * 128) // 128
                            for ic in range(NICM):
                                kb.t(lambda e, o=ps_m[par],
                                     l=he[:, ic * HF + tc * 128:ic * HF + (tc + 1) * 128],
                                     r=wedb[ring][:, ic * JBM:(ic + 1) * JBM],
                                     st=(ic == 0), sp=(ic == NICM - 1):
                                     e.matmul(o[:, :], l, r, start=st, stop=sp),
                                     waits=[(("gwd0", "gwd1")[ring], c_d), ("vv", m_he), ("vv", d_war[par])],
                                     mark=(ic == NICM - 1))
                            m_mm = kb.cnt["pp"]
                            if tc == NTCH - 1:
                                wd_war[ring] = m_mm
                            kb.v(lambda e, o=acc[:, tc * D + jb * JBM:tc * D + (jb + 1) * JBM],
                                 i=ps_m[par], s1=w_all[:, tcg * 8 + ex:tcg * 8 + ex + 1],
                                 a2=acc[:, tc * D + jb * JBM:tc * D + (jb + 1) * JBM]:
                                 e.scalar_tensor_tensor(o, i[:, :], s1, a2, ALU.mult, ALU.add),
                                 waits=[("pp", m_mm), ("sr2", c_acc)])
                            d_war[par] = kb.cnt["vv"]
                    he_war = max(wd_war)
                m_done = kb.cnt["vv"]
                for tc in range(NTCH):
                    trow = hbase + tc * 128
                    kb.dma("sync", "sout",
                           lambda e, o=out_t[trow:trow + 128, :], i=acc[:, tc * D:(tc + 1) * D]:
                           e.dma_start(out=o, in_=i),
                           waits=[("vv", m_done)])
            kb._op("sync", lambda e: e.nop(), [("sout", kb.cnt["sout"])], None, 1)
            kb.flush(barrier=False)

    return nc


def _prep_core_inputs(core, x_np, Wq, Wk, Wv, Wo, Wr, Wsg, Wsgu, Wsd, Weg, Wed, T):
    b = core // 4
    hq = core % 4
    n_own = T // 4
    own0 = hq * n_own
    cs = hq * 512
    f32 = np.float32
    strip = np.zeros((128, 896), f32)
    for kk in range(128):
        strip[kk, :kk + 384] = NEGBIG
    oh = np.zeros((128, 4), f32)
    oh[:, hq] = 1.0

    def img(a, nch=16):
        n = a.shape[1]
        return np.ascontiguousarray(
            a.reshape(nch, 128, n).transpose(1, 0, 2).reshape(128, nch * n))

    wsgu_t = Wsgu.reshape(16, 128, 2 * NICS_C, 128).transpose(2, 1, 0, 3) \
        .reshape(2 * NICS_C * 128, 16 * 128)
    wsd_t = Wsd.reshape(NICS_C, 128, NJBS_C, JBS_C).transpose(2, 1, 0, 3) \
        .reshape(NJBS_C * 128, NICS_C * JBS_C)
    weg_t = Weg.reshape(NEXP, 16, 128, 2 * 11, 128).transpose(0, 3, 2, 1, 4) \
        .reshape(NEXP * 2 * 11 * 128, 16 * 128)
    wed_t = Wed.reshape(NEXP, 11, 128, NJBM_C, JBM_C).transpose(0, 3, 2, 1, 4) \
        .reshape(NEXP * NJBM_C * 128, 11 * JBM_C)
    d = {
        "x": np.ascontiguousarray(x_np[b, :T], f32),
        "x_own": np.ascontiguousarray(x_np[b, own0:own0 + n_own], f32),
        "wq": img(np.ascontiguousarray(Wq[cs:cs + 512, :].T)),
        "wk": img(np.ascontiguousarray(Wk[cs:cs + 512, :].T)),
        "wv": img(np.ascontiguousarray(Wv[cs:cs + 512, :].T)),
        "wos": img(np.ascontiguousarray(Wo.T)),
        "wrsg": img(np.ascontiguousarray(np.concatenate([Wr.T, Wsg.T], axis=1))),
        "wsgu": np.ascontiguousarray(wsgu_t),
        "wsd": np.ascontiguousarray(wsd_t),
        "weg": np.ascontiguousarray(weg_t),
        "wed": np.ascontiguousarray(wed_t),
        "ident": np.eye(128, dtype=f32),
        "ones_mat": np.ones((128, 128), f32),
        "strip": strip,
        "oh": oh,
    }
    return d


def kernel(L0_attn_x, L0_attn_Wq, L0_attn_Wk, L0_attn_Wv, L0_attn_Wo,
           L0_ffn_W_router, L0_ffn_W_sgate, L0_ffn_Ws_gate_up, L0_ffn_Ws_down,
           L0_ffn_We_gate_up, L0_ffn_We_down):
    T = 8192
    n_own = T // 4
    x = np.asarray(L0_attn_x, np.float32)
    args = (np.asarray(L0_attn_Wq, np.float32), np.asarray(L0_attn_Wk, np.float32),
            np.asarray(L0_attn_Wv, np.float32), np.asarray(L0_attn_Wo, np.float32),
            np.asarray(L0_ffn_W_router, np.float32), np.asarray(L0_ffn_W_sgate, np.float32),
            np.asarray(L0_ffn_Ws_gate_up, np.float32), np.asarray(L0_ffn_Ws_down, np.float32),
            np.asarray(L0_ffn_We_gate_up, np.float32), np.asarray(L0_ffn_We_down, np.float32))
    in_maps = [_prep_core_inputs(c, x, *args, T) for c in range(8)]
    nc = build_nc(T=T)
    res = run_bass_kernel_spmd(nc, in_maps, core_ids=list(range(8)))
    out = np.empty((2, T, D), np.float32)
    for c in range(8):
        b = c // 4
        own0 = (c % 4) * n_own
        out[b, own0:own0 + n_own] = res.results[c]["out"]
    return out
